# version 18
# speedup vs baseline: 1.2959x; 1.0010x over previous
import numpy as np
import ml_dtypes
from contextlib import ExitStack

import concourse.bass as bass
import concourse.mybir as mybir
from concourse.bass_utils import run_bass_kernel_spmd

F32 = mybir.dt.float32
BF16 = mybir.dt.bfloat16
AF = mybir.ActivationFunctionType
ALU = mybir.AluOpType
AX = mybir.AxisListType

NCORE = 8
B = 2
S = 8192
D = 1024
T = S // 4
NTT = T // 128
NQT = T // 512
NDC = D // 128
EPS = 1e-5

bf = ml_dtypes.bfloat16



class TBuf:
    __slots__ = ("name", "writer", "readers")

    def __init__(self, name=""):
        self.name = name
        self.writer = None
        self.readers = []


class Prog:
    SECT = {"tensor": "tensor", "vector": "vector", "scalar": "scalar",
            "pool": "gpsimd", "sync": "sync", "gdma": "gpsimd"}
    COMPUTE = ("tensor", "vector", "scalar", "pool")
    DMAQ = ("sync", "gdma")

    def __init__(self, nc, es, n_dma_sems=8):
        self.nc = nc
        self.es = es
        self.ops = {"tensor": [], "vector": [], "scalar": [], "gpsimd": [], "sync": []}
        self.sems = {}
        self.counts = {}
        for s in self.COMPUTE:
            self._mksem("c_" + s)
        self.dma_pool = {}
        self.dma_rr = {}
        self.dma_out = {}
        for q in self.DMAQ:
            self.dma_pool[q] = []
            self.dma_rr[q] = 0
            for i in range(n_dma_sems):
                nm = f"d_{q}{i}"
                self._mksem(nm)
                self.dma_pool[q].append(nm)
                self.dma_out[nm] = None

    def _mksem(self, name):
        h = self.es.enter_context(self.nc.semaphore(name))
        self.sems[name] = h
        self.counts[name] = 0

    def _deps_from(self, reads, writes, sreads, deps):
        d = []
        for b in reads:
            if b.writer is not None:
                d.append((b.writer, False))
        for b in sreads:
            if b.writer is not None:
                d.append((b.writer, True))
        for b in writes:
            if b.writer is not None:
                d.append((b.writer, False))
            for r in b.readers:
                d.append((r, False))
        for t in deps:
            d.append((t, False))
        return d

    def _record(self, section, stream_sem, emit, dlist, inc, tok):
        self.ops[section].append((emit, dlist, stream_sem, inc, tok))

    def op(self, stream, emit, reads=(), writes=(), sreads=(), deps=(),
           wreads=()):
        sem = "c_" + stream
        section = self.SECT[stream]
        dlist = self._deps_from(reads, writes, sreads, deps)
        for b in wreads:
            if b.writer is not None:
                dlist.append((b.writer, None))
        tok = (sem, self.counts[sem] + 1, stream)
        self.counts[sem] += 1
        self._record(section, sem, emit, dlist, 1, tok)
        for b in reads:
            b.readers.append(tok)
        for b in wreads:
            b.readers.append(tok)
        for b in sreads:
            b.readers.append(tok)
        for b in writes:
            b.writer = tok
            b.readers = []
        return tok

    def dma(self, queue, out_ap, in_ap, reads=(), writes=(), deps=(), **kw):
        pool = self.dma_pool[queue]
        sem = pool[self.dma_rr[queue] % len(pool)]
        self.dma_rr[queue] += 1
        dlist = self._deps_from(reads, writes, (), deps)
        prev = self.dma_out[sem]
        if prev is not None:
            dlist.append((prev, False))
        section = self.SECT[queue]
        tok = (sem, self.counts[sem] + 16, queue)
        self.counts[sem] += 16

        def emit(eng, out_ap=out_ap, in_ap=in_ap, kw=kw):
            return eng.dma_start(out=out_ap, in_=in_ap, **kw)

        self._record(section, sem, emit, dlist, 16, tok)
        self.dma_out[sem] = tok
        for b in reads:
            b.readers.append(tok)
        for b in writes:
            b.writer = tok
            b.readers = []
        return tok

    def raw(self, stream, emit, reads=(), writes=(), sreads=(), deps=()):
        return self.op(stream, emit, reads, writes, sreads, deps)

    def barrier_tokens(self):
        toks = []
        for s in self.COMPUTE:
            sem = "c_" + s
            if self.counts[sem] > 0:
                toks.append((sem, self.counts[sem], s))
        for q in self.DMAQ:
            for semn in self.dma_pool[q]:
                if self.counts[semn] > 0:
                    toks.append((semn, self.counts[semn], q))
        return toks

    def finalize(self, final_waits=()):
        nc = self.nc
        with nc.Block() as block:
            def mk(section):
                myops = self.ops[section]

                def body(eng):
                    waited = {}
                    for (emit, dlist, sem, inc, tok) in myops:
                        own_stream = tok[2]
                        for (dtok, strong) in dlist:
                            dsem, dval, dstream = dtok
                            if (not strong and dstream == "tensor"
                                    and own_stream == "tensor"):
                                continue
                            if strong is None and dstream == own_stream:
                                continue
                            if waited.get(dsem, -1) >= dval:
                                continue
                            eng.wait_ge(self.sems[dsem], dval)
                            waited[dsem] = dval
                        ins = emit(eng)
                        ins.then_inc(self.sems[sem], inc)
                    if section == "sync":
                        for (dsem, dval, _s) in final_waits:
                            eng.wait_ge(self.sems[dsem], dval)
                return body

            block.sync(mk("sync"))
            block.gpsimd(mk("gpsimd"))
            block.scalar(mk("scalar"))
            block.vector(mk("vector"))
            block.tensor(mk("tensor"))



class Ctx:

    def __init__(self, name):
        self.nc = bass.Bass("TRN2", target_bir_lowering=False)
        self.es = ExitStack()
        self.prog = Prog(self.nc, self.es)
        self.sb = {}
        self.ps = {}
        self.dram = {}
        self.host = {}

    def dram_in(self, name, shape, dtype, provider):
        ap = self.nc.dram_tensor(name, list(shape), dtype, kind="ExternalInput").ap()
        self.dram[name] = ap
        self.host[name] = provider
        return ap

    def dram_out(self, name, shape, dtype):
        ap = self.nc.dram_tensor(name, list(shape), dtype, kind="ExternalOutput").ap()
        self.dram[name] = ap
        return ap

    def sbuf(self, name, shape, dtype):
        t = self.es.enter_context(self.nc.sbuf_tensor("s_" + name, list(shape), dtype))
        b = TBuf(name)
        self.sb[name] = (t, b)
        return t, b

    def psum(self, name, shape, dtype=F32):
        t = self.es.enter_context(self.nc.psum_tensor("p_" + name, list(shape), dtype))
        b = TBuf(name)
        self.ps[name] = (t, b)
        return t, b

    def epsb(self):
        if "epsb" not in self.sb:
            t, b = self.sbuf("epsb", (128, 1), F32)
            self.prog.op("vector", lambda v: v.memset(t[:, :], EPS), writes=[b])
        return self.sb["epsb"]

    def in_maps(self, inputs, n=NCORE, extra=None):
        out = []
        for c in range(n):
            m = {}
            for k, f in self.host.items():
                if f is None:
                    m[k] = np.ascontiguousarray(extra[k][c])
                else:
                    m[k] = np.ascontiguousarray(f(c, inputs))
            out.append(m)
        return out


def const_identity_bf16():
    return np.eye(128, dtype=bf)


def const_identity_f32():
    return np.eye(128, dtype=np.float32)


def const_masks():
    m = np.zeros((4, 128, 512), dtype=bf)
    i = np.arange(128)[:, None]
    j = np.arange(512)[None, :]
    for k in range(4):
        m[k] = ((i + k * 128) <= j).astype(bf)
    return m.reshape(4 * 128, 512)


def w_fm_view(w):
    K, N = w.shape
    return np.ascontiguousarray(w.reshape(K // 128, 128, N).transpose(1, 0, 2)
                                .reshape(128, (K // 128) * N))


def vec_fm_view(v):
    Dv = v.shape[0]
    return np.ascontiguousarray(v.reshape(Dv // 128, 128).T)



def load_weight_fm(ctx, dname, sname, K, N, provider, dtype=BF16):
    ap = ctx.dram_in(dname, (128, (K // 128) * N), F32, provider)
    t, b = ctx.sbuf(sname, (128, (K // 128) * N), dtype)
    ctx.prog.dma("gdma", t[:, :], ap[:, :], writes=[b])
    return t, b, N


def load_vec_fm(ctx, dname, sname, Dv, provider):
    prov = None if provider is None else (
        lambda c, ii, p=provider: vec_fm_view(p(c, ii)))
    ap = ctx.dram_in(dname, (128, Dv // 128), F32, prov)
    t, b = ctx.sbuf(sname, (128, Dv // 128), F32)
    ctx.prog.dma("sync", t[:, :], ap[:, :], writes=[b])
    return t, b


def mm_fm(ctx, psname, wt, wb, kc_stride, nkc, mc, rhs_t, rhs_b, rhs_col0, ntok,
          extra_deps=()):
    p = ctx.prog
    pt, pb = ctx.ps[psname]
    toks = []
    for kc in range(nkc):
        lhs = wt[:, kc * kc_stride + mc * 128: kc * kc_stride + mc * 128 + 128]
        rhs = rhs_t[:, kc * T + rhs_col0: kc * T + rhs_col0 + ntok]
        tok = p.op("tensor",
                   lambda t, lhs=lhs, rhs=rhs, st=(kc == 0), sp=(kc == nkc - 1),
                          pt=pt, ntok=ntok:
                   t.matmul(out=pt[:, 0:ntok], lhsT=lhs, rhs=rhs, start=st, stop=sp),
                   reads=[wb, rhs_b], writes=([pb] if kc == 0 else []),
                   deps=(list(extra_deps) if kc == 0 else []))
        if kc > 0:
            pb.writer = tok
        toks.append(tok)
    return toks


def transpose128(ctx, psname, in_ap, in_b, ident_t, ident_b, dtype_tag="bf"):
    p = ctx.prog
    pt, pb = ctx.ps[psname]
    tok = p.op("tensor",
               lambda t, pt=pt, in_ap=in_ap: t.transpose(
                   out=pt[:, 0:128], in_=in_ap, identity=ident_t[:, :]),
               reads=[in_b, ident_b], writes=[pb])
    return pt, pb, tok


def ln_stats_tok(ctx, x_t, x_b, stat_t, stat_b, sq_t, sq_b, i, nfree=D):
    p = ctx.prog
    c0 = i * 8
    xs = x_t[:, i * nfree:(i + 1) * nfree]
    p.op("vector", lambda v: v.tensor_reduce(
        out=stat_t[:, c0 + 3:c0 + 4], in_=xs, axis=AX.X, op=ALU.add),
        reads=[x_b], writes=[stat_b])
    p.op("scalar", lambda s: s.activation(
        out=sq_t[:, 0:nfree], in_=xs, func=AF.Square,
        accum_out=stat_t[:, c0 + 4:c0 + 5]),
        reads=[x_b], writes=[sq_b, stat_b])
    p.op("vector", lambda v: v.tensor_scalar_mul(
        out=stat_t[:, c0 + 0:c0 + 1], in0=stat_t[:, c0 + 3:c0 + 4],
        scalar1=1.0 / nfree), reads=[stat_b], writes=[stat_b])
    p.op("vector", lambda v: v.tensor_tensor(
        out=stat_t[:, c0 + 5:c0 + 6], in0=stat_t[:, c0 + 0:c0 + 1],
        in1=stat_t[:, c0 + 0:c0 + 1], op=ALU.mult),
        reads=[stat_b], writes=[stat_b])
    p.op("vector", lambda v: v.scalar_tensor_tensor(
        out=stat_t[:, c0 + 5:c0 + 6], in0=stat_t[:, c0 + 3 + 1:c0 + 5],
        scalar=1.0 / nfree, in1=stat_t[:, c0 + 5:c0 + 6],
        op0=ALU.mult, op1=ALU.subtract),
        sreads=[stat_b], reads=[stat_b], writes=[stat_b])
    p.op("scalar", lambda s: s.activation(
        out=stat_t[:, c0 + 6:c0 + 7], in_=stat_t[:, c0 + 5:c0 + 6],
        func=AF.Sqrt, bias=EPS, scale=1.0), reads=[stat_b], writes=[stat_b])
    p.op("vector", lambda v: v.reciprocal(
        out=stat_t[:, c0 + 1:c0 + 2], in_=stat_t[:, c0 + 6:c0 + 7]),
        reads=[stat_b], writes=[stat_b])
    p.op("vector", lambda v: v.tensor_tensor(
        out=stat_t[:, c0 + 7:c0 + 8], in0=stat_t[:, c0 + 1:c0 + 2],
        in1=stat_t[:, c0 + 1:c0 + 2], op=ALU.mult),
        reads=[stat_b], writes=[stat_b])
    p.op("vector", lambda v: v.tensor_tensor(
        out=stat_t[:, c0 + 7:c0 + 8], in0=stat_t[:, c0 + 7:c0 + 8],
        in1=stat_t[:, c0 + 5:c0 + 6], op=ALU.mult),
        reads=[stat_b], writes=[stat_b])
    p.op("scalar", lambda s: s.activation(
        out=stat_t[:, c0 + 6:c0 + 7], in_=stat_t[:, c0 + 7:c0 + 8],
        func=AF.Sqrt, bias=ept[:, 0:1], scale=1.0),
        reads=[stat_b], sreads=[epb], writes=[stat_b])
    p.op("vector", lambda v: v.reciprocal(
        out=stat_t[:, c0 + 2:c0 + 3], in_=stat_t[:, c0 + 6:c0 + 7]),
        reads=[stat_b], writes=[stat_b])




def emit_xn_pipeline(ctx, x_ap, idb_t, idb_b):
    p = ctx.prog
    stat_t, stat_b = ctx.sbuf("stat", (128, NTT * 8), F32)
    sq_t, sq_b = ctx.sbuf("sqscr", (128, D), BF16)
    xnfm_t, xnfm_b = ctx.sbuf("xn_fm", (128, NDC * T), BF16)
    xst = [ctx.sbuf(f"xstg{k}", (128, D), F32) for k in range(2)]
    xnt = [ctx.sbuf(f"xntile{k}", (128, D), BF16) for k in range(2)]
    for i in range(NTT):
        xs_t, xs_b = xst[i % 2]
        xn_t, xn_b = xnt[i % 2]
        p.dma("sync", xs_t[:, :], x_ap[:, i * D:(i + 1) * D], writes=[xs_b])
        _ln_stats_tile(ctx, xs_t, xs_b, stat_t, stat_b, sq_t, sq_b, i)
        c0 = i * 8
        p.op("vector", lambda v, xs_t=xs_t, xs_b=xs_b, xn_t=xn_t, c0=c0: v.tensor_scalar(
            out=xn_t[:, :], in0=xs_t[:, :],
            scalar1=stat_t[:, c0 + 0:c0 + 1], scalar2=stat_t[:, c0 + 1:c0 + 2],
            op0=ALU.subtract, op1=ALU.mult),
            reads=[xs_b], sreads=[stat_b], writes=[xn_b])
        for c in range(NDC):
            pt, pb = ctx.ps[("tpa", "tpb")[c % 2]]
            p.op("tensor", lambda t, xn_t=xn_t, c=c, pt=pt: t.transpose(
                out=pt[:, 0:128], in_=xn_t[:, c * 128:(c + 1) * 128],
                identity=idb_t[:, :]),
                reads=[xn_b, idb_b], writes=[pb])
            p.op("vector", lambda v, i=i, c=c, pt=pt: v.tensor_copy(
                out=xnfm_t[:, c * T + i * 128:c * T + i * 128 + 128], in_=pt[:, 0:128]),
                reads=[pb], writes=[xnfm_b])
    return stat_t, stat_b, xnfm_t, xnfm_b


def _ln_stats_tile(ctx, xs_t, xs_b, stat_t, stat_b, sq_t, sq_b, i, nfree=D):
    p = ctx.prog
    c0 = i * 8
    xs = xs_t[:, 0:nfree]
    p.op("vector", lambda v: v.tensor_reduce(
        out=stat_t[:, c0 + 3:c0 + 4], in_=xs, axis=AX.X, op=ALU.add),
        reads=[xs_b], writes=[stat_b])
    p.op("scalar", lambda s: s.activation(
        out=sq_t[:, 0:nfree], in_=xs, func=AF.Square,
        accum_out=stat_t[:, c0 + 4:c0 + 5]),
        reads=[xs_b], writes=[sq_b, stat_b])
    p.op("vector", lambda v: v.tensor_scalar_mul(
        out=stat_t[:, c0 + 0:c0 + 1], in0=stat_t[:, c0 + 3:c0 + 4],
        scalar1=1.0 / nfree), reads=[stat_b], writes=[stat_b])
    p.op("vector", lambda v: v.tensor_tensor(
        out=stat_t[:, c0 + 5:c0 + 6], in0=stat_t[:, c0 + 0:c0 + 1],
        in1=stat_t[:, c0 + 0:c0 + 1], op=ALU.mult),
        reads=[stat_b], writes=[stat_b])
    p.op("vector", lambda v: v.scalar_tensor_tensor(
        out=stat_t[:, c0 + 5:c0 + 6], in0=stat_t[:, c0 + 4:c0 + 5],
        scalar=1.0 / nfree, in1=stat_t[:, c0 + 5:c0 + 6],
        op0=ALU.mult, op1=ALU.subtract),
        reads=[stat_b], writes=[stat_b])
    ept, epb = ctx.epsb()
    p.op("scalar", lambda s: s.activation(
        out=stat_t[:, c0 + 6:c0 + 7], in_=stat_t[:, c0 + 5:c0 + 6],
        func=AF.Sqrt, bias=ept[:, 0:1], scale=1.0),
        reads=[stat_b], sreads=[epb], writes=[stat_b])
    p.op("vector", lambda v: v.reciprocal(
        out=stat_t[:, c0 + 1:c0 + 2], in_=stat_t[:, c0 + 6:c0 + 7]),
        reads=[stat_b], writes=[stat_b])
    p.op("vector", lambda v: v.tensor_tensor(
        out=stat_t[:, c0 + 7:c0 + 8], in0=stat_t[:, c0 + 1:c0 + 2],
        in1=stat_t[:, c0 + 1:c0 + 2], op=ALU.mult),
        reads=[stat_b], writes=[stat_b])
    p.op("vector", lambda v: v.tensor_tensor(
        out=stat_t[:, c0 + 7:c0 + 8], in0=stat_t[:, c0 + 7:c0 + 8],
        in1=stat_t[:, c0 + 5:c0 + 6], op=ALU.mult),
        reads=[stat_b], writes=[stat_b])
    p.op("scalar", lambda s: s.activation(
        out=stat_t[:, c0 + 6:c0 + 7], in_=stat_t[:, c0 + 7:c0 + 8],
        func=AF.Sqrt, bias=ept[:, 0:1], scale=1.0),
        reads=[stat_b], sreads=[epb], writes=[stat_b])
    p.op("vector", lambda v: v.reciprocal(
        out=stat_t[:, c0 + 2:c0 + 3], in_=stat_t[:, c0 + 6:c0 + 7]),
        reads=[stat_b], writes=[stat_b])


def emit_affine_fm_inplace(ctx, xnfm_t, xnfm_b, r1_t, r1_b, g_t, g_b, b_t, b_b,
                           scr_t, scr_b):
    p = ctx.prog
    for mc in range(NDC):
        p.op("vector", lambda v, mc=mc: v.tensor_tensor(
            out=scr_t[:, :], in0=xnfm_t[:, mc * T:(mc + 1) * T], in1=r1_t[:, :],
            op=ALU.mult), reads=[xnfm_b, r1_b], writes=[scr_b])
        p.op("vector", lambda v, mc=mc: v.tensor_scalar(
            out=xnfm_t[:, mc * T:(mc + 1) * T], in0=scr_t[:, :],
            scalar1=g_t[:, mc:mc + 1], scalar2=b_t[:, mc:mc + 1],
            op0=ALU.mult, op1=ALU.add),
            reads=[scr_b], sreads=[g_b, b_b], writes=[xnfm_b])


def emit_headmax_chunk(ctx, src_ap, src_b, maxbuf_t, maxbuf_b, col,
                       ones2_t, ones2_b, sq_t, sq_b, ps_name="hm"):
    p = ctx.prog
    pt, pb = ctx.ps[ps_name]
    p.op("scalar", lambda s: s.activation(
        out=sq_t[:, 0:512], in_=src_ap, func=AF.Square),
        reads=[src_b], writes=[sq_b])
    p.op("tensor", lambda t: t.matmul(
        out=pt[0:2, 0:512], lhsT=ones2_t[:, 0:2], rhs=sq_t[:, 0:512],
        start=True, stop=True), reads=[sq_b, ones2_b], writes=[pb])
    p.op("vector", lambda v: v.tensor_reduce(
        out=maxbuf_t[0:2, col:col + 1], in_=pt[0:2, 0:512],
        axis=AX.X, op=ALU.max), reads=[pb], writes=[maxbuf_b])


def emit_bcast_row(ctx, stat_t, stat_b, col_off, dst_t, dst_b, ones_t, ones_b,
                   idf_t, idf_b, tag, ps_name="tpf"):
    p = ctx.prog
    bt_t, bt_b = ctx.sbuf("btt_" + tag, (128, 128), F32)
    for i in range(NTT):
        c0 = i * 8
        p.op("vector", lambda v, c0=c0: v.tensor_scalar_mul(
            out=bt_t[:, :], in0=ones_t[:, 0:128],
            scalar1=stat_t[:, c0 + col_off:c0 + col_off + 1]),
            reads=[ones_b], sreads=[stat_b], writes=[bt_b])
        pt, pb = ctx.ps[ps_name]
        p.op("tensor", lambda t, pt=pt: t.transpose(
            out=pt[:, 0:128], in_=bt_t[:, :], identity=idf_t[:, :]),
            reads=[bt_b, idf_b], writes=[pb])
        p.op("vector", lambda v, i=i, pt=pt: v.tensor_copy(
            out=dst_t[:, i * 128:(i + 1) * 128], in_=pt[:, 0:128]),
            reads=[pb], writes=[dst_b])


def emit_proj(ctx, tag, w_ap, Nout, rhs_t, rhs_b, biasv, out_dram,
              scale_mcs=(), scale=1.0, hm=None, out_sbuf=None, act=None,
              stage_names=None):
    p = ctx.prog
    nmc = Nout // 128
    wmc = [ctx.sbuf(f"wmc_{tag}{k}", (128, NDC * 128), BF16) for k in range(2)]
    if out_sbuf is None:
        stg = [ctx.sbuf(stage_names[k] if stage_names else f"stg_{tag}{k}",
                        (128, T), BF16) for k in range(2)]
    fn = act or AF.Identity
    for mc in range(nmc):
        wt, wb = wmc[mc % 2]
        p.dma("gdma", wt[:, :].rearrange("p (kc n) -> p kc n", kc=NDC),
              w_ap.rearrange("p (kc n) -> p kc n", kc=NDC)[:, :, mc * 128:mc * 128 + 128],
              writes=[wb])
        if out_sbuf is None:
            st_t, st_b = stg[mc % 2]
        for tt in range(NQT):
            psn = ("pma", "pmb")[tt % 2]
            mm_fm(ctx, psn, wt, wb, 128, NDC, 0, rhs_t, rhs_b, tt * 512, 512)
            pt, pb = ctx.ps[psn]
            sc = scale if mc in scale_mcs else 1.0
            if out_sbuf is not None:
                o_t, o_b = out_sbuf
                dst = o_t[:, mc * T + tt * 512:mc * T + (tt + 1) * 512 - 0]
                wrb = o_b
            else:
                dst = st_t[:, tt * 512:(tt + 1) * 512]
                wrb = st_b
            if biasv is not None:
                bt, bb = biasv
                p.op("scalar", lambda s, pt=pt, mc=mc, dst=dst, sc=sc: s.activation(
                    out=dst, in_=pt[:, 0:512],
                    func=fn, bias=bt[:, mc:mc + 1], scale=sc),
                    reads=[pb], sreads=[bb], writes=[wrb])
            else:
                p.op("scalar", lambda s, pt=pt, dst=dst, sc=sc: s.activation(
                    out=dst, in_=pt[:, 0:512], func=fn, scale=sc),
                    reads=[pb], writes=[wrb])
            for hmi in (hm or []):
                if hmi["lo"] <= mc < hmi["hi"]:
                    mt, mb = hmi["maxbuf"]
                    o2t, o2b = hmi["ones2"]
                    sqt, sqb = hmi["sq"]
                    emit_headmax_chunk(ctx, dst, wrb,
                                       mt, mb,
                                       hmi["colbase"] + (mc - hmi["lo"]) * NQT + tt,
                                       o2t, o2b, sqt, sqb)
        if out_sbuf is None:
            p.dma("sync", out_dram[:, mc * T:(mc + 1) * T], st_t[:, :], reads=[st_b])


def emit_maxreduce(ctx, maxbuf_t, maxbuf_b, out_t, out_b, nmc):
    p = ctx.prog
    for mc in range(nmc):
        p.op("vector", lambda v, mc=mc: v.tensor_reduce(
            out=out_t[0:2, mc:mc + 1], in_=maxbuf_t[0:2, mc * NQT:(mc + 1) * NQT],
            axis=AX.X, op=ALU.max), reads=[maxbuf_b], writes=[out_b])



PRE_CFG = {
    0: dict(scale=0.125, has_bias=True, affine=False, ffn=True, l0affine=True),
    1: dict(scale=1.0 / np.sqrt(128.0), has_bias=True, affine=False, ffn=False),
    2: dict(scale=0.125, has_bias=False, affine=True, ffn=False),
    3: dict(scale=0.125, has_bias=True, affine=False, ffn=False),
}


def pre_w_provider(layer):
    def f(c, ii):
        if layer == 0:
            w = np.concatenate([ii["L0_attn_Wq"], ii["L0_attn_Wk"], ii["L0_attn_Wv"]], axis=1)
        elif layer == 1:
            w = np.concatenate([ii["L1_attn_Wq"], ii["L1_attn_Wk"], ii["L1_attn_Wv"]], axis=1)
        elif layer == 2:
            w = ii["L2_attn_w_qkv"].T
        else:
            wk = ii["L3_attn_Wkqv"]
            w3 = wk.reshape(16, 1024, 64, 3)
            k = w3[..., 0].transpose(1, 0, 2).reshape(1024, 1024)
            q = w3[..., 1].transpose(1, 0, 2).reshape(1024, 1024)
            v = w3[..., 2].transpose(1, 0, 2).reshape(1024, 1024)
            w = np.concatenate([k, q, v], axis=1)
        return w_fm_view(np.ascontiguousarray(w))
    return f


def pre_b_provider(layer):
    def f(c, ii):
        if layer == 0:
            b = np.concatenate([ii["L0_attn_bq"], ii["L0_attn_bk"], ii["L0_attn_bv"]])
        elif layer == 1:
            b = np.concatenate([ii["L1_attn_bq"], ii["L1_attn_bk"], ii["L1_attn_bv"]])
        else:
            bk = ii["L3_attn_bkqv"].reshape(16, 64, 3)
            b = np.concatenate([np.ascontiguousarray(bk[:, :, j]).reshape(1024)
                                for j in (0, 1, 2)])
        return b
    return f


def build_pre(layer, x_source):
    cfg = PRE_CFG[layer]
    ctx = Ctx(f"pre{layer}")
    p = ctx.prog

    x_ap = ctx.dram_in("x", (128, NTT * D), F32, x_source)
    idb_ap = ctx.dram_in("idb", (128, 128), F32, lambda c, ii: const_identity_f32())
    o_xn = ctx.dram_out("o_xn", (128, NDC * T), BF16)
    o_p = ctx.dram_out("o_p", (128, 24 * T), BF16)
    o_qmax = ctx.dram_out("o_qmax", (2, NDC), F32)
    o_kmax = ctx.dram_out("o_kmax", (2, NDC), F32)

    for nm, dt in [("tpa", BF16), ("tpb", BF16), ("tpf", F32), ("pma", F32),
                   ("pmb", F32), ("hm", F32)]:
        ctx.psum(nm, (128, 512), dt)

    idb_t, idb_b = ctx.sbuf("idb", (128, 128), BF16)
    idf_t, idf_b = ctx.sbuf("idf", (128, 128), F32)
    ones_t, ones_b = ctx.sbuf("ones", (128, 128), F32)
    ones2_t, ones2_b = ctx.sbuf("ones2", (128, 2), BF16)
    p.dma("gdma", idb_t[:, :], idb_ap[:, :], writes=[idb_b])
    p.dma("sync", idf_t[:, :], idb_ap[:, :], writes=[idf_b])
    p.op("vector", lambda v: v.memset(ones_t[:, :], 1.0), writes=[ones_b])
    p.op("vector", lambda v: v.memset(ones2_t[0:64, 0:1], 1.0), writes=[ones2_b])
    p.op("vector", lambda v: v.memset(ones2_t[64:128, 0:1], 0.0), writes=[ones2_b])
    p.op("vector", lambda v: v.memset(ones2_t[0:64, 1:2], 0.0), writes=[ones2_b])
    p.op("vector", lambda v: v.memset(ones2_t[64:128, 1:2], 1.0), writes=[ones2_b])

    stat_t, stat_b, xnfm_t, xnfm_b = emit_xn_pipeline(ctx, x_ap, idb_t, idb_b)
    p.dma("sync", o_xn[:, :], xnfm_t[:, :], reads=[xnfm_b])

    if cfg["ffn"]:
        o_ffn = ctx.dram_out("o_ffn", (128, NDC * T), F32)
        h1 = ctx.sbuf("h1", (128, NDC * T), BF16)
        h2 = ctx.sbuf("h2", (128, NDC * T), BF16)
        fb_t, fb_b = load_vec_fm(ctx, "b_ffn", "bs_ffn", D, lambda c, ii: ii["L0_ffn_bias"])
        w1_ap = ctx.dram_in("w_ffn", (128, NDC * D), F32,
                            lambda c, ii: w_fm_view(ii["L0_ffn_weight"]))
        wa1_ap = ctx.dram_in("w_fa1", (128, NDC * D), F32,
                             lambda c, ii: w_fm_view(ii["L0_ffn_aux_weight1"]))
        wa2_ap = ctx.dram_in("w_fa2", (128, NDC * D), F32,
                             lambda c, ii: w_fm_view(ii["L0_ffn_aux_weight2"]))
        emit_proj(ctx, "f1", w1_ap, D, xnfm_t, xnfm_b, None, None, out_sbuf=h1)
        emit_proj(ctx, "f2", wa1_ap, D, h1[0], h1[1], None, None, out_sbuf=h2)
        fst = [ctx.sbuf(f"fst{k}", (128, T), F32) for k in range(2)]
        wmcf = [ctx.sbuf(f"wmcf{k}", (128, NDC * 128), BF16) for k in range(2)]
        for mc in range(NDC):
            wt, wb = wmcf[mc % 2]
            p.dma("gdma", wt[:, :].rearrange("p (kc n) -> p kc n", kc=NDC),
                  wa2_ap.rearrange("p (kc n) -> p kc n", kc=NDC)[:, :, mc * 128:mc * 128 + 128],
                  writes=[wb])
            st_t, st_b = fst[mc % 2]
            for tt in range(NQT):
                psn = ("pma", "pmb")[tt % 2]
                mm_fm(ctx, psn, wt, wb, 128, NDC, 0, h2[0], h2[1], tt * 512, 512)
                pt, pb = ctx.ps[psn]
                p.op("scalar", lambda s, pt=pt, mc=mc, tt=tt, st_t=st_t: s.activation(
                    out=st_t[:, tt * 512:(tt + 1) * 512], in_=pt[:, 0:512],
                    func=AF.Tanh, bias=fb_t[:, mc:mc + 1], scale=1.0),
                    reads=[pb], sreads=[fb_b], writes=[st_b])
            p.dma("sync", o_ffn[:, mc * T:(mc + 1) * T], st_t[:, :], reads=[st_b])

    if cfg.get("l0affine"):
        r1_t, r1_b = ctx.sbuf("r1", (128, T), F32)
        emit_bcast_row(ctx, stat_t, stat_b, 2, r1_t, r1_b, ones_t, ones_b,
                       idf_t, idf_b, "r1")
        g1 = load_vec_fm(ctx, "g1", "g1s", D, lambda c, ii: ii["L0_attn_ln1_g"])
        b1 = load_vec_fm(ctx, "b1", "b1s", D, lambda c, ii: ii["L0_attn_ln1_b"])
        scr_t, scr_b = ctx.sbuf("scr", (128, T), F32)
        emit_affine_fm_inplace(ctx, xnfm_t, xnfm_b, r1_t, r1_b, g1[0], g1[1],
                               b1[0], b1[1], scr_t, scr_b)

    if cfg["affine"]:
        o_xn2 = ctx.dram_out("o_xn2", (128, NDC * T), BF16)
        r1_t, r1_b = ctx.sbuf("r1", (128, T), F32)
        emit_bcast_row(ctx, stat_t, stat_b, 2, r1_t, r1_b, ones_t, ones_b,
                       idf_t, idf_b, "r1")
        g1 = load_vec_fm(ctx, "g2f", "g2fs", D, lambda c, ii: ii["L2_ffn_gamma"])
        b1 = load_vec_fm(ctx, "b2f", "b2fs", D, lambda c, ii: ii["L2_ffn_beta"])
        scr_t, scr_b = ctx.sbuf("scr", (128, T), F32)
        x2st = [ctx.sbuf(f"x2st{k}", (128, T), BF16) for k in range(2)]
        for mc in range(NDC):
            st_t, st_b = x2st[mc % 2]
            p.op("vector", lambda v, mc=mc: v.tensor_tensor(
                out=scr_t[:, :], in0=xnfm_t[:, mc * T:(mc + 1) * T], in1=r1_t[:, :],
                op=ALU.mult), reads=[xnfm_b, r1_b], writes=[scr_b])
            p.op("vector", lambda v, mc=mc, st_t=st_t: v.tensor_scalar(
                out=st_t[:, :], in0=scr_t[:, :],
                scalar1=g1[0][:, mc:mc + 1], scalar2=b1[0][:, mc:mc + 1],
                op0=ALU.mult, op1=ALU.add),
                reads=[scr_b], sreads=[g1[1], b1[1]], writes=[st_b])
            p.dma("sync", o_xn2[:, mc * T:(mc + 1) * T], st_t[:, :], reads=[st_b])

    w_ap = ctx.dram_in("w_p", (128, NDC * 3072), F32, pre_w_provider(layer))
    biasv = None
    if cfg["has_bias"]:
        bt, bb = load_vec_fm(ctx, "b_p", "bs_p", 3072, pre_b_provider(layer))
        p.op("vector", lambda v: v.tensor_scalar_mul(
            out=bt[:, 0:8], in0=bt[:, 0:8], scalar1=cfg["scale"]),
            reads=[bb], writes=[bb])
        biasv = (bt, bb)
    sqh = ctx.sbuf("sqh", (128, 512), BF16)
    maxq = ctx.sbuf("maxq", (2, NDC * NQT), F32)
    maxk = ctx.sbuf("maxk", (2, NDC * NQT), F32)
    qmax = ctx.sbuf("qmax", (2, NDC), F32)
    kmax = ctx.sbuf("kmax", (2, NDC), F32)
    hmq = dict(maxbuf=maxq, colbase=0, ones2=(ones2_t, ones2_b), sq=sqh, lo=0, hi=8)
    hmk = dict(maxbuf=maxk, colbase=0, ones2=(ones2_t, ones2_b), sq=sqh, lo=8, hi=16)

    emit_proj(ctx, "p", w_ap, 3072, xnfm_t, xnfm_b, biasv, o_p,
              scale_mcs=set(range(8)), scale=cfg["scale"],
              hm=[hmq, hmk])
    emit_maxreduce(ctx, maxq[0], maxq[1], qmax[0], qmax[1], NDC)
    emit_maxreduce(ctx, maxk[0], maxk[1], kmax[0], kmax[1], NDC)
    p.dma("sync", o_qmax[:, :], qmax[0][0:2, :], reads=[qmax[1]])
    p.dma("sync", o_kmax[:, :], kmax[0][0:2, :], reads=[kmax[1]])

    p.finalize(final_waits=p.barrier_tokens())
    return ctx


def run_ctx(ctx, inputs, n=NCORE, extra=None):
    maps = ctx.in_maps(inputs, n, extra)
    res = run_bass_kernel_spmd(ctx.nc, maps, core_ids=list(range(n)))
    return res.results


def x_source_from_key(key):
    def f(c, ii):
        b, q = c // 4, c % 4
        return ii[key][b, q * T:(q + 1) * T, :].astype(np.float32) \
            .reshape(NTT, 128, D).transpose(1, 0, 2).reshape(128, NTT * D)
    return f


def x_source_from_host(arrs):
    def f(c, ii):
        return arrs[c].reshape(NTT, 128, D).transpose(1, 0, 2).reshape(128, NTT * D)
    return f


def fm_to_mat(a, T_=T):
    nmc = a.shape[1] // T_
    return np.ascontiguousarray(
        a.reshape(128, nmc, T_).transpose(2, 1, 0).reshape(T_, nmc * 128))


def mat_to_fm(m, T_=None):
    T_ = T_ if T_ is not None else m.shape[0]
    N = m.shape[1]
    return np.ascontiguousarray(
        m.reshape(T_, N // 128, 128).transpose(2, 1, 0).reshape(128, (N // 128) * T_))



ATTN_CFG = {
    0: dict(nh=16, hd=64, causal=True, veff=False),
    1: dict(nh=8, hd=128, causal=False, veff=True),
    2: dict(nh=16, hd=64, causal=False, veff=False),
    3: dict(nh=16, hd=64, causal=True, veff=False),
}


def build_attn(layer):
    cfg = ATTN_CFG[layer]
    nh, hd, causal, veff = cfg["nh"], cfg["hd"], cfg["causal"], cfg["veff"]
    ctx = Ctx(f"attn{layer}")
    p = ctx.prog

    q_ap = ctx.dram_in("q", (128, NDC * T), BF16, None)
    kf_ap = ctx.dram_in("kf", (128, NDC * S), BF16, None)
    vt_ap = ctx.dram_in("vt", (S, D), BF16, None)
    qmax_ap = ctx.dram_in("qmax", (2, 8), F32, None)
    kmax_ap = ctx.dram_in("kmaxall", (2, 32), F32, None)
    o_z = ctx.dram_out("o_z", (128, NDC * T), F32)
    if causal:
        mask_ap = ctx.dram_in("masks", (128, 4 * 512), F32,
                              lambda c, ii: const_masks().astype(np.float32))

    for nm in ["out0", "out1", "sums", "nrm"]:
        ctx.psum(nm, (128, 512), F32)
    for nm in ["scp0", "scp1"]:
        ctx.psum(nm, (128, 1024), F32)

    q_t, q_b = ctx.sbuf("q", (128, NDC * T), BF16)
    p.dma("sync", q_t[:, :], q_ap[:, :], writes=[q_b])
    attn_t, attn_b = ctx.sbuf("attn", (128, NDC * T), BF16)
    ones_t, ones_b = ctx.sbuf("ones", (128, 128), F32)
    onesb_t, onesb_b = ctx.sbuf("onesb", (128, 1), BF16)
    p.op("vector", lambda v: v.memset(ones_t[:, :], 1.0), writes=[ones_b])
    p.op("vector", lambda v: v.memset(onesb_t[:, :], 1.0), writes=[onesb_b])
    if causal:
        mask_t, mask_b = ctx.sbuf("masks", (128, 4 * 512), BF16)
        p.dma("gdma", mask_t[:, :], mask_ap[:, :], writes=[mask_b])

    qm_t, qm_b = ctx.sbuf("qm", (2, 8), F32)
    km_t, km_b = ctx.sbuf("km", (2, 32), F32)
    p.dma("sync", qm_t[0:2, :], qmax_ap[:, :], writes=[qm_b])
    p.dma("sync", km_t[0:2, :], kmax_ap[:, :], writes=[km_b])
    mrow_t, mrow_b = ctx.sbuf("mrow", (2, 8), F32)
    for mc in range(8):
        p.op("vector", lambda v, mc=mc: v.tensor_reduce(
            out=mrow_t[0:2, mc:mc + 1], in_=km_t[0:2, mc * 4:(mc + 1) * 4],
            axis=AX.X, op=ALU.max), reads=[km_b], writes=[mrow_b])
    p.op("vector", lambda v: v.tensor_tensor(
        out=mrow_t[0:2, :], in0=mrow_t[0:2, :], in1=qm_t[0:2, :], op=ALU.mult),
        reads=[mrow_b, qm_b], writes=[mrow_b])
    p.op("scalar", lambda s: s.activation(
        out=mrow_t[0:2, :], in_=mrow_t[0:2, :], func=AF.Sqrt, scale=1.0),
        reads=[mrow_b], writes=[mrow_b])
    p.op("vector", lambda v: v.tensor_scalar(
        out=mrow_t[0:2, :], in0=mrow_t[0:2, :], scalar1=-1.0, scalar2=-1.0,
        op0=ALU.mult, op1=ALU.add), reads=[mrow_b], writes=[mrow_b])
    sel_ap = ctx.dram_in("selmat", (2, 2), F32,
                         lambda c, ii: np.array([[0.0, 1.0], [1.0, 0.0]], np.float32))
    sel_t, sel_b = ctx.sbuf("sel", (2, 2), F32)
    p.dma("sync", sel_t[0:2, :], sel_ap[:, :], writes=[sel_b])
    nmr_t, nmr_b = ctx.sbuf("nmr", (2, 16), F32)
    pt, pb = ctx.ps["nrm"]
    p.op("tensor", lambda t: t.matmul(out=pt[0:1, 0:8], lhsT=sel_t[0:2, 1:2],
                                      rhs=mrow_t[0:2, 0:8], start=True, stop=True),
         reads=[sel_b, mrow_b], writes=[pb])
    p.op("vector", lambda v: v.tensor_copy(out=nmr_t[0:1, 0:8], in_=pt[0:1, 0:8]),
         reads=[pb], writes=[nmr_b])
    p.op("tensor", lambda t: t.matmul(out=pt[0:1, 0:8], lhsT=sel_t[0:2, 0:1],
                                      rhs=mrow_t[0:2, 0:8], start=True, stop=True),
         reads=[sel_b, mrow_b], writes=[pb])
    p.op("vector", lambda v: v.tensor_copy(out=nmr_t[0:1, 8:16], in_=pt[0:1, 0:8]),
         reads=[pb], writes=[nmr_b])
    negMb_t, negMb_b = ctx.sbuf("negMb", (128, nh), F32)
    for h in range(nh):
        src_col = (h % 2) * 8 + h // 2 if nh == 16 else (h % 2) * 8 + h // 2
        pt2, pb2 = ctx.ps["nrm"]
        p.op("tensor", lambda t, h=h, src_col=src_col, pt2=pt2: t.matmul(
            out=pt2[:, 0:1], lhsT=ones_t[0:1, :], rhs=nmr_t[0:1, src_col:src_col + 1],
            start=True, stop=True), reads=[ones_b, nmr_b], writes=[pb2])
        p.op("vector", lambda v, h=h, pt2=pt2: v.tensor_copy(
            out=negMb_t[:, h:h + 1], in_=pt2[:, 0:1]), reads=[pb2], writes=[negMb_b])

    if veff:
        lq1 = ctx.dram_in("lq1", (128, 1), F32, None)
        lk1 = ctx.dram_in("lk1", (128, 1), F32, None)
        lq2 = ctx.dram_in("lq2", (128, 1), F32, None)
        lk2 = ctx.dram_in("lk2", (128, 1), F32, None)
        lam_sb = {}
        for nm, ap in [("lq1", lq1), ("lk1", lk1), ("lq2", lq2), ("lk2", lk2)]:
            t, b_ = ctx.sbuf("l_" + nm, (128, 1), F32)
            p.dma("sync", t[:, :], ap[:, :], writes=[b_])
            lam_sb[nm] = (t, b_)
        lsc_t, lsc_b = ctx.sbuf("lsc", (1, 4), F32)
        pt3, pb3 = ctx.ps["nrm"]
        for j, (a, bb) in enumerate([("lq1", "lk1"), ("lq2", "lk2")]):
            p.op("tensor", lambda t, a=a, bb=bb, pt3=pt3: t.matmul(
                out=pt3[0:1, 0:1], lhsT=lam_sb[a][0][:, 0:1], rhs=lam_sb[bb][0][:, 0:1],
                start=True, stop=True),
                reads=[lam_sb[a][1], lam_sb[bb][1]], writes=[pb3])
            p.op("scalar", lambda s, j=j, pt3=pt3: s.activation(
                out=lsc_t[0:1, j:j + 1], in_=pt3[0:1, 0:1], func=AF.Exp, scale=1.0),
                reads=[pb3], writes=[lsc_b])
        p.op("vector", lambda v: v.tensor_tensor(
            out=lsc_t[0:1, 2:3], in0=lsc_t[0:1, 1:2], in1=lsc_t[0:1, 0:1],
            op=ALU.subtract), reads=[lsc_b], writes=[lsc_b])
        p.op("vector", lambda v: v.tensor_scalar_add(
            out=lsc_t[0:1, 2:3], in0=lsc_t[0:1, 2:3], scalar1=-0.8),
            reads=[lsc_b], writes=[lsc_b])
        neglam_t, neglam_b = ctx.sbuf("neglam", (128, 1), F32)
        p.op("tensor", lambda t, pt3=pt3: t.matmul(
            out=pt3[:, 0:1], lhsT=ones_t[0:1, :], rhs=lsc_t[0:1, 2:3],
            start=True, stop=True), reads=[ones_b, lsc_b], writes=[pb3])
        p.op("vector", lambda v, pt3=pt3: v.tensor_copy(
            out=neglam_t[:, :], in_=pt3[:, 0:1]), reads=[pb3], writes=[neglam_b])

    nhg = nh if hd == 128 else nh // 2
    perhead = hd + 1
    nvb = 4 if veff else (2 if hd == 64 else 1)
    chunk_w = nvb * perhead
    NCH = S // 128
    if veff:
        v65_t, v65_b = ctx.sbuf("v65", (128, NCH * 4 * 128), BF16)
        vst = [ctx.sbuf(f"vst{k}", (128, D), BF16) for k in range(2)]
        for c in range(NCH):
            vs_t, vs_b = vst[c % 2]
            p.dma("gdma", vs_t[:, :], vt_ap[c * 128:(c + 1) * 128, :], writes=[vs_b])
            for j in range(4):
                p.op("vector", lambda v, c=c, j=j, vs_t=vs_t: v.scalar_tensor_tensor(
                    out=v65_t[:, c * 512 + j * 128:c * 512 + j * 128 + 128],
                    in0=vs_t[:, (j + 4) * 128:(j + 5) * 128],
                    scalar=neglam_t[:, 0:1],
                    in1=vs_t[:, j * 128:(j + 1) * 128],
                    op0=ALU.mult, op1=ALU.add),
                    reads=[vs_b], sreads=[neglam_b], writes=[v65_b])

    u_bufs = [ctx.sbuf(f"u{k}", (128, 1024), BF16) for k in range(2 if veff else 3)]
    rr_t, rr_b = ctx.sbuf("rr", (1, 512), F32)
    nrm_t, nrm_b = ctx.sbuf("nrmsb", (128, 512), F32)
    kh = [ctx.sbuf(f"kh{k}", (128, S), BF16) for k in range(1 if veff else 2)]
    if not veff:
        v65g = [ctx.sbuf(f"v65g{k}", (128, NCH * chunk_w), BF16) for k in range(2)]
    if causal:
        dt_ap = ctx.dram_in("dtile", (128, 512), F32,
                            lambda c, ii: (np.arange(128)[:, None]
                                           - np.arange(512)[None, :]).astype(np.float32))
        tk_ap = ctx.dram_in("tok0col", (128, 1), F32, None)
        dt_t, dt_b = ctx.sbuf("dtile", (128, 512), BF16)
        tk_t, tk_b = ctx.sbuf("tok0col", (128, 1), F32)
        p.dma("gdma", dt_t[:, :], dt_ap[:, :], writes=[dt_b])
        p.dma("sync", tk_t[:, :], tk_ap[:, :], writes=[tk_b])
        thr_t, thr_b = ctx.sbuf("thr", (128, 1), F32)
        cm_t, cm_b = ctx.sbuf("cmask", (128, 1024), BF16)

    def emit_mask(qt, c, slot):
        off = float(c * 128 - qt * 512)
        p.op("vector", lambda v: v.tensor_scalar_add(
            out=thr_t[:, :], in0=tk_t[:, :], scalar1=-off),
            reads=[tk_b], writes=[thr_b])
        p.op("vector", lambda v, slot=slot: v.tensor_single_scalar(
            out=cm_t[:, slot * 512:(slot + 1) * 512], in_=dt_t[:, :],
            scalar=thr_t[:, 0:1], op=ALU.is_le),
            reads=[dt_b], sreads=[thr_b], writes=[cm_b])

    def emit_norm_write(h, qt, out_ps):
        opt, opb = out_ps
        smt, smb = ctx.ps["sums"]
        if hd == 64:
            st_, sb_, srow = opt, opb, 64
        else:
            st_, sb_, srow = smt, smb, 0
        p.op("scalar", lambda s, st_=st_, srow=srow: s.activation(
            out=rr_t[0:1, :], in_=st_[srow:srow + 1, 0:512], func=AF.Ln, scale=1.0),
            reads=[sb_], writes=[rr_b])
        p.op("scalar", lambda s: s.activation(
            out=rr_t[0:1, :], in_=rr_t[0:1, :], func=AF.Exp, scale=-1.0),
            wreads=[rr_b], writes=[rr_b])
        npt, npb = ctx.ps["nrm"]
        p.op("tensor", lambda t, npt=npt: t.matmul(
            out=npt[0:hd, 0:512], lhsT=ones_t[0:1, 0:hd], rhs=rr_t[0:1, :],
            start=True, stop=True), reads=[ones_b, rr_b], writes=[npb])
        p.op("vector", lambda v, npt=npt: v.tensor_copy(
            out=nrm_t[0:hd, :], in_=npt[0:hd, 0:512]),
            reads=[npb], writes=[nrm_b])
        arow = (h % 2) * 64 if hd == 64 else 0
        acol = (h // 2) * T if hd == 64 else h * T
        p.op("vector", lambda v, opt=opt, opb=opb, arow=arow, acol=acol, qt=qt:
             v.tensor_tensor(
                 out=attn_t[arow:arow + hd, acol + qt * 512:acol + qt * 512 + 512],
                 in0=opt[0:hd, 0:512], in1=nrm_t[0:hd, :], op=ALU.mult),
             reads=[opb, nrm_b], writes=[attn_b])

    def emit_group_qt(g, qt, kh_pair, v65_pair):
        kh_t, kh_b = kh_pair
        v65t, v65b = v65_pair
        nch_eff = min(NCH, 52 + 4 * qt) if causal else NCH
        njs = 2 if hd == 64 else 1
        for cp in range(0, nch_eff, 2):
            cs = [cp] + ([cp + 1] if cp + 1 < nch_eff else [])
            nw = len(cs) * 512
            if causal:
                masked = [c for c in cs if c >= 4 * qt]
                for c in masked:
                    emit_mask(qt, c, cs.index(c))
            for j in range(njs):
                h = 2 * g + j if hd == 64 else g
                k = (cp // 2) * njs + j
                scp, scpb = ctx.ps[("scp0", "scp1")[k % 2]]
                opt, opb = ctx.ps[("out0", "out1")[j]] if hd == 64 else ctx.ps["out0"]
                smt, smb = ctx.ps["sums"]
                qrow = (h % 2) * 64 if hd == 64 else 0
                qcol = (h // 2) * T if hd == 64 else h * T
                for idx, c in enumerate(cs):
                    lhs = kh_t[:, c * 128:(c + 1) * 128] if hd == 128 else \
                        kh_t[j * 64:(j + 1) * 64, c * 128:(c + 1) * 128]
                    rhs = q_t[qrow:qrow + hd, qcol + qt * 512:qcol + qt * 512 + 512]
                    tok = p.op("tensor", lambda t, lhs=lhs, rhs=rhs, scp=scp, idx=idx:
                               t.matmul(out=scp[:, idx * 512:(idx + 1) * 512],
                                        lhsT=lhs, rhs=rhs, start=True, stop=True),
                               reads=[kh_b, q_b], writes=([scpb] if idx == 0 else []))
                    scpb.writer = tok
                u_t, u_b = u_bufs[k % len(u_bufs)]
                p.op("scalar", lambda s, scp=scp, u_t=u_t, h=h, nw=nw: s.activation(
                    out=u_t[:, 0:nw], in_=scp[:, 0:nw], func=AF.Exp,
                    bias=negMb_t[:, h:h + 1], scale=1.0),
                    reads=[scpb], sreads=[negMb_b], writes=[u_b])
                if causal and any(c >= 4 * qt for c in cs):
                    lo = min(idx for idx, c in enumerate(cs) if c >= 4 * qt) * 512
                    p.op("vector", lambda v, u_t=u_t, lo=lo, nw=nw: v.tensor_tensor(
                        out=u_t[:, lo:nw], in0=u_t[:, lo:nw], in1=cm_t[:, lo:nw],
                        op=ALU.mult), reads=[u_b, cm_b], writes=[u_b])
                for idx, c in enumerate(cs):
                    first = (c == 0)
                    last = (c == nch_eff - 1)
                    if veff:
                        lhsv = v65t[:, c * 512 + (h % 4) * 128:
                                    c * 512 + (h % 4) * 128 + 128]
                    else:
                        lhsv = v65t[:, c * chunk_w + j * perhead:
                                    c * chunk_w + (j + 1) * perhead]
                    urhs = u_t[:, idx * 512:(idx + 1) * 512]
                    if hd == 64:
                        tok = p.op("tensor", lambda t, lhsv=lhsv, urhs=urhs, opt=opt,
                                   first=first, last=last:
                                   t.matmul(out=opt[0:65, 0:512], lhsT=lhsv, rhs=urhs,
                                            start=first, stop=last),
                                   reads=[v65b, u_b], writes=([opb] if first else []))
                        opb.writer = tok
                    else:
                        tok = p.op("tensor", lambda t, lhsv=lhsv, urhs=urhs, opt=opt,
                                   first=first, last=last:
                                   t.matmul(out=opt[:, 0:512], lhsT=lhsv, rhs=urhs,
                                            start=first, stop=last),
                                   reads=[v65b, u_b], writes=([opb] if first else []))
                        opb.writer = tok
                        tok2 = p.op("tensor", lambda t, urhs=urhs, first=first,
                                    last=last, smt=smt: t.matmul(
                            out=smt[0:1, 0:512], lhsT=onesb_t[:, 0:1], rhs=urhs,
                            start=first, stop=last),
                            reads=[onesb_b, u_b], writes=([smb] if first else []))
                        smb.writer = tok2
        for j in range(njs):
            h = 2 * g + j if hd == 64 else g
            emit_norm_write(h, qt,
                            ctx.ps[("out0", "out1")[j]] if hd == 64 else ctx.ps["out0"])

    def emit_flash():
        for qt in range(NQT):
            for g in range(nhg):
                kh_pair = kh[g % len(kh)]
                p.dma("sync", kh_pair[0][:, :], kf_ap[:, g * S:(g + 1) * S],
                      writes=[kh_pair[1]])
                if not veff:
                    vp = v65g[g % 2]
                    nheads_in_g = 2 if hd == 64 else 1
                    hbase = g * nheads_in_g
                    for j in range(nvb):
                        p.dma("gdma",
                              vp[0][:, :].rearrange("p (c n) -> p c n", c=NCH)[
                                  :, :, j * perhead:j * perhead + hd],
                              vt_ap.rearrange("(c kk) (hh n) -> kk c hh n",
                                              kk=128, n=hd)[:, :, hbase + j, :],
                              writes=[vp[1]])
                    for j in range(nvb):
                        p.op("vector", lambda v, vp=vp, j=j: v.memset(
                            vp[0][:, j * perhead + hd::chunk_w], 1.0), writes=[vp[1]])
                    v65_pair = vp
                else:
                    v65_pair = (v65_t, v65_b)
                emit_group_qt(g, qt, kh_pair, v65_pair)

    ctx.emit_flash = emit_flash
    ctx.attn = (attn_t, attn_b)
    return ctx


def finish_attn(ctx, layer):
    p = ctx.prog
    attn_t, attn_b = ctx.attn
    o_z = ctx.dram["o_z"]
    ctx.emit_flash()

    if layer == 1:
        onesc_t, onesc_b = ctx.sbuf("onesc", (128, 1), BF16)
        onescf_t, onescf_b = ctx.sbuf("onescf", (128, 1), BF16)
        p.op("vector", lambda v: v.memset(onesc_t[:, :], 1.0), writes=[onesc_b])
        p.op("vector", lambda v: v.memset(onescf_t[:, :], 1.0), writes=[onescf_b])
        sqs = ctx.sbuf("sqs", (128, 512), BF16)
        rows_t, rows_b = ctx.sbuf("rows", (1, 2 * T), F32)
        scr_t, scr_b = ctx.sbuf("lnscr", (128, T), F32)
        _emit_fm_lnstats(ctx, attn_t, attn_b, rows_t, rows_b,
                         (onesc_t, onesc_b), (onescf_t, onescf_b), sqs,
                         ctx.ps["sums"], ctx.ps["nrm"])
        _rows_to_r(ctx, rows_t, rows_b)
        ones_t, ones_b = ctx.sb["ones"]
        Rb = ctx.sbuf("Rb", (128, T), F32)
        RMb = ctx.sbuf("RMb", (128, T), F32)
        _bcast_rows(ctx, rows_t, rows_b, Rb, RMb, ones_t, ones_b, "nrm")
        lnw = load_vec_fm(ctx, "lnw", "lnws", D, None)
        lnb = load_vec_fm(ctx, "lnb", "lnbs", D, None)
        p.op("vector", lambda v: v.tensor_scalar_mul(
            out=lnw[0][:, :], in0=lnw[0][:, :], scalar1=0.2),
            reads=[lnw[1]], writes=[lnw[1]])
        p.op("vector", lambda v: v.tensor_scalar_mul(
            out=lnb[0][:, :], in0=lnb[0][:, :], scalar1=0.2),
            reads=[lnb[1]], writes=[lnb[1]])
        for mc in range(NDC):
            p.op("vector", lambda v, mc=mc: v.tensor_tensor(
                out=scr_t[:, :], in0=attn_t[:, mc * T:(mc + 1) * T], in1=Rb[0][:, :],
                op=ALU.mult), reads=[attn_b, Rb[1]], writes=[scr_b])
            p.op("vector", lambda v: v.tensor_tensor(
                out=scr_t[:, :], in0=scr_t[:, :], in1=RMb[0][:, :], op=ALU.add),
                reads=[scr_b, RMb[1]], writes=[scr_b])
            p.op("vector", lambda v, mc=mc: v.tensor_scalar(
                out=attn_t[:, mc * T:(mc + 1) * T], in0=scr_t[:, :],
                scalar1=lnw[0][:, mc:mc + 1], scalar2=lnb[0][:, mc:mc + 1],
                op0=ALU.mult, op1=ALU.add),
                reads=[scr_b], sreads=[lnw[1], lnb[1]], writes=[attn_b])

    w_ap = ctx.dram_in("w_o", (128, NDC * D), F32, None)
    bo = load_vec_fm(ctx, "b_o", "bos", D, None)
    if layer == 0:
        xn_ap = ctx.dram_in("xn", (128, NDC * T), BF16, None)
        xnst = [ctx.sbuf(f"xnst{k}", (128, 512), BF16) for k in range(2)]
    wmo = [ctx.sbuf(f"wmo{k}", (128, NDC * 128), BF16) for k in range(2)]
    nzst = 1 if layer == 1 else 2
    zw = 1024 if layer == 1 else T
    zst = [ctx.sbuf(f"zst{k}", (128, zw), F32) for k in range(nzst)]
    for mc in range(NDC):
        wt, wb = wmo[mc % 2]
        p.dma("gdma", wt[:, :].rearrange("p (kc n) -> p kc n", kc=NDC),
              w_ap.rearrange("p (kc n) -> p kc n", kc=NDC)[:, :, mc * 128:mc * 128 + 128],
              writes=[wb])
        st_t, st_b = zst[mc % nzst]
        for tt in range(NQT):
            stcol = (tt % (zw // 512)) * 512
            psn = ("out0", "out1")[tt % 2]
            mm_fm(ctx, psn, wt, wb, 128, NDC, 0, attn_t, attn_b, tt * 512, 512)
            pt, pb = ctx.ps[psn]
            if layer == 0:
                xs_t, xs_b = xnst[tt % 2]
                p.dma("sync", xs_t[:, :],
                      xn_ap[:, mc * T + tt * 512:mc * T + (tt + 1) * 512],
                      writes=[xs_b])
                p.op("vector", lambda v, pt=pt, mc=mc, stcol=stcol, st_t=st_t,
                     xs_t=xs_t, xs_b=xs_b: v.scalar_tensor_tensor(
                         out=st_t[:, stcol:stcol + 512], in0=pt[:, 0:512],
                         scalar=bo[0][:, mc:mc + 1], in1=xs_t[:, :],
                         op0=ALU.add, op1=ALU.add),
                     reads=[pb, xs_b], sreads=[bo[1]], writes=[st_b])
            else:
                p.op("scalar", lambda s, pt=pt, mc=mc, stcol=stcol, st_t=st_t:
                     s.activation(
                         out=st_t[:, stcol:stcol + 512], in_=pt[:, 0:512],
                         func=AF.Identity, bias=bo[0][:, mc:mc + 1], scale=1.0),
                     reads=[pb], sreads=[bo[1]], writes=[st_b])
            if stcol + 512 == zw:
                p.dma("sync", o_z[:, mc * T + (tt + 1) * 512 - zw:
                               mc * T + (tt + 1) * 512], st_t[:, 0:zw],
                      reads=[st_b])

    p.finalize(final_waits=p.barrier_tokens())
    return ctx



def attn_extra(layer, pre_res, inputs):
    kf = {}
    vt = {}
    km = {}
    for b in range(2):
        cores = [b * 4 + i for i in range(4)]
        kf[b] = np.concatenate(
            [pre_res[c]["o_p"][:, 8 * T:16 * T].reshape(128, NDC, T) for c in cores],
            axis=2).reshape(128, NDC * S)
        vt[b] = np.concatenate(
            [fm_to_mat(pre_res[c]["o_p"][:, 16 * T:24 * T]) for c in cores], axis=0)
        km[b] = np.stack([pre_res[c]["o_kmax"] for c in cores], axis=-1).reshape(2, 32)
    extra = {
        "q": [pre_res[c]["o_p"][:, 0:8 * T] for c in range(NCORE)],
        "kf": [kf[c // 4] for c in range(NCORE)],
        "vt": [vt[c // 4] for c in range(NCORE)],
        "qmax": [pre_res[c]["o_qmax"] for c in range(NCORE)],
        "kmaxall": [km[c // 4] for c in range(NCORE)],
    }
    if ATTN_CFG[layer]["causal"]:
        extra["tok0col"] = [np.full((128, 1), (c % 4) * T, np.float32)
                            for c in range(NCORE)]
    if layer == 0:
        extra["xn"] = [pre_res[c]["o_xn"] for c in range(NCORE)]
        extra["w_o"] = [w_fm_view(inputs["L0_attn_Wo"])] * NCORE
        extra["b_o"] = [vec_fm_view(inputs["L0_attn_bo"])] * NCORE
    elif layer == 1:
        extra["w_o"] = [w_fm_view(inputs["L1_attn_Wo"])] * NCORE
        extra["b_o"] = [vec_fm_view(inputs["L1_attn_bo"])] * NCORE
        extra["lnw"] = [vec_fm_view(inputs["L1_attn_ln_w"])] * NCORE
        extra["lnb"] = [vec_fm_view(inputs["L1_attn_ln_b"])] * NCORE
        for nm, key in [("lq1", "L1_attn_lambda_q1"), ("lk1", "L1_attn_lambda_k1"),
                        ("lq2", "L1_attn_lambda_q2"), ("lk2", "L1_attn_lambda_k2")]:
            extra[nm] = [inputs[key].reshape(128, 1).astype(np.float32)] * NCORE
    elif layer == 2:
        extra["w_o"] = [w_fm_view(np.ascontiguousarray(inputs["L2_attn_w_proj"].T))] * NCORE
        extra["b_o"] = [vec_fm_view(inputs["L2_attn_b_proj"])] * NCORE
    else:
        extra["w_o"] = [w_fm_view(inputs["L3_attn_Wproj"])] * NCORE
        extra["b_o"] = [vec_fm_view(inputs["L3_attn_bproj"])] * NCORE
    return extra



def build_mamba_front():
    ctx = Ctx("mfront")
    p = ctx.prog
    z_ap = ctx.dram_in("z", (128, NDC * T), F32, None)
    inw_ap = ctx.dram_in("inw", (128, NDC * 4096), F32,
                         lambda c, ii: w_fm_view(ii["L0_attn_in_w"]))
    g2 = load_vec_fm(ctx, "g2", "g2s", D, lambda c, ii: ii["L0_attn_ln2_g"])
    b2 = load_vec_fm(ctx, "b2", "b2s", D, lambda c, ii: ii["L0_attn_ln2_b"])
    inb = load_vec_fm(ctx, "inb", "inbs", 4096, lambda c, ii: ii["L0_attn_in_b"])
    cw_ap = ctx.dram_in("convw", (128, 16 * 4), F32,
                        lambda c, ii: ii["L0_attn_conv_w"].reshape(16, 128, 4)
                        .transpose(1, 0, 2).reshape(128, 64))
    cb = load_vec_fm(ctx, "convb", "convbs", 2048, lambda c, ii: ii["L0_attn_conv_b"])
    xpw_ap = ctx.dram_in("xpw", (128, 16 * 64), F32, lambda c, ii: _xpw_pad(ii))
    xpb_ap = ctx.dram_in("xpb", (64, 1), F32,
                         lambda c, ii: np.concatenate(
                             [ii["L0_attn_xp_b"][0:16], np.zeros(16, np.float32),
                              ii["L0_attn_xp_b"][16:32], np.zeros(16, np.float32)]
                         ).reshape(64, 1))
    o_xdbl = ctx.dram_out("o_xdbl", (64, T), F32)
    o_res = ctx.dram_out("o_res", (128, 16 * T), BF16)
    o_xzb = ctx.dram_out("o_xzb", (128, 16 * 6), F32)

    for nm in ["pma", "pmb", "xd0", "xd1", "xd2", "xd3", "pr"]:
        ctx.psum(nm, (128, 512), F32)

    z_t, z_b = ctx.sbuf("z", (128, NDC * T), F32)
    p.dma("sync", z_t[:, :], z_ap[:, :], writes=[z_b])
    ones_t, ones_b = ctx.sbuf("ones", (128, 128), F32)
    onesc_t, onesc_b = ctx.sbuf("onesc", (128, 1), F32)
    onescb_t, onescb_b = ctx.sbuf("onescb", (128, 1), BF16)
    p.op("vector", lambda v: v.memset(ones_t[:, :], 1.0), writes=[ones_b])
    p.op("vector", lambda v: v.memset(onesc_t[:, :], 1.0), writes=[onesc_b])
    p.op("vector", lambda v: v.memset(onescb_t[:, :], 1.0), writes=[onescb_b])
    cw_t, cw_b = ctx.sbuf("convw", (128, 64), F32)
    p.dma("sync", cw_t[:, :], cw_ap[:, :], writes=[cw_b])
    xpw_t, xpw_b = ctx.sbuf("xpw", (128, 16 * 64), BF16)
    p.dma("gdma", xpw_t[:, :], xpw_ap[:, :], writes=[xpw_b])
    xpb_t, xpb_b = ctx.sbuf("xpb", (64, 1), F32)
    p.dma("sync", xpb_t[0:64, :], xpb_ap[:, :], writes=[xpb_b])

    sqs = ctx.sbuf("sqs", (128, 512), BF16)
    rows_t, rows_b = ctx.sbuf("rows", (1, 2 * T), F32)
    pt, pb = ctx.ps["pr"]
    pt2, pb2 = ctx.ps["xd0"]
    for tt in range(NQT):
        for mc in range(NDC):
            sl = z_t[:, mc * T + tt * 512:mc * T + (tt + 1) * 512]
            tok = p.op("tensor", lambda t, sl=sl, mc=mc: t.matmul(
                out=pt[0:1, 0:512], lhsT=onesc_t[:, 0:1], rhs=sl,
                start=(mc == 0), stop=(mc == NDC - 1)),
                reads=[z_b, onesc_b], writes=([pb] if mc == 0 else []))
            pb.writer = tok
            p.op("scalar", lambda s, sl=sl: s.activation(
                out=sqs[0][:, :], in_=sl, func=AF.Square),
                reads=[z_b], writes=[sqs[1]])
            tok = p.op("tensor", lambda t, mc=mc: t.matmul(
                out=pt2[0:1, 0:512], lhsT=onescb_t[:, 0:1], rhs=sqs[0][:, :],
                start=(mc == 0), stop=(mc == NDC - 1)),
                reads=[sqs[1], onescb_b], writes=([pb2] if mc == 0 else []))
            pb2.writer = tok
        p.op("vector", lambda v, tt=tt: v.tensor_copy(
            out=rows_t[0:1, tt * 512:(tt + 1) * 512], in_=pt[0:1, 0:512]),
            reads=[pb], writes=[rows_b])
        p.op("vector", lambda v, tt=tt: v.tensor_copy(
            out=rows_t[0:1, T + tt * 512:T + (tt + 1) * 512], in_=pt2[0:1, 0:512]),
            reads=[pb2], writes=[rows_b])
    _rows_to_r(ctx, rows_t, rows_b)
    Rb = ctx.sbuf("Rb", (128, T), F32)
    RMb = ctx.sbuf("RMb", (128, T), F32)
    _bcast_rows(ctx, rows_t, rows_b, Rb, RMb, ones_t, ones_b, "pr")

    zn_t, zn_b = ctx.sbuf("zn", (128, NDC * T), BF16)
    scr_t, scr_b = ctx.sbuf("scr", (128, T), F32)
    for mc in range(NDC):
        p.op("vector", lambda v, mc=mc: v.tensor_tensor(
            out=scr_t[:, :], in0=z_t[:, mc * T:(mc + 1) * T], in1=Rb[0][:, :],
            op=ALU.mult), reads=[z_b, Rb[1]], writes=[scr_b])
        p.op("vector", lambda v: v.tensor_tensor(
            out=scr_t[:, :], in0=scr_t[:, :], in1=RMb[0][:, :], op=ALU.add),
            reads=[scr_b, RMb[1]], writes=[scr_b])
        p.op("vector", lambda v, mc=mc: v.tensor_scalar(
            out=zn_t[:, mc * T:(mc + 1) * T], in0=scr_t[:, :],
            scalar1=g2[0][:, mc:mc + 1], scalar2=b2[0][:, mc:mc + 1],
            op0=ALU.mult, op1=ALU.add),
            reads=[scr_b], sreads=[g2[1], b2[1]], writes=[zn_b])

    wmc = [ctx.sbuf(f"wmc{k}", (128, NDC * 128), BF16) for k in range(2)]
    xmc_t, xmc_b = ctx.sbuf("xmc", (128, T), F32)
    cac_t, cac_b = ctx.sbuf("cac", (128, T), F32)
    xms_t, xms_b = ctx.sbuf("xms", (128, T), BF16)
    rst = [ctx.sbuf(f"rst{k}", (128, T), BF16) for k in range(2)]
    xdbl_sb = ctx.sbuf("xdbl", (64, T), F32)
    for oc in range(32):
        wt, wb = wmc[oc % 2]
        p.dma("gdma", wt[:, :].rearrange("p (kc n) -> p kc n", kc=NDC),
              inw_ap.rearrange("p (kc n) -> p kc n", kc=NDC)[:, :, oc * 128:oc * 128 + 128],
              writes=[wb])
        if oc < 16:
            for tt in range(NQT):
                psn = ("pma", "pmb")[tt % 2]
                mm_fm(ctx, psn, wt, wb, 128, NDC, 0, zn_t, zn_b, tt * 512, 512)
                pz, pzb = ctx.ps[psn]
                p.op("scalar", lambda s, pz=pz, oc=oc, tt=tt: s.activation(
                    out=xmc_t[:, tt * 512:(tt + 1) * 512], in_=pz[:, 0:512],
                    func=AF.Identity, bias=inb[0][:, oc:oc + 1], scale=1.0),
                    reads=[pzb], sreads=[inb[1]], writes=[xmc_b])
            p.dma("sync", o_xzb[:, oc * 6:oc * 6 + 3], xmc_t[:, 0:3], reads=[xmc_b])
            p.dma("sync", o_xzb[:, oc * 6 + 3:oc * 6 + 6], xmc_t[:, T - 3:T],
                  reads=[xmc_b])
            p.op("vector", lambda v, oc=oc: v.tensor_scalar_mul(
                out=cac_t[:, :], in0=xmc_t[:, :], scalar1=cw_t[:, oc * 4 + 3:oc * 4 + 4]),
                reads=[xmc_b], sreads=[cw_b], writes=[cac_b])
            for kk in (2, 1, 0):
                sh = 3 - kk
                p.op("vector", lambda v, oc=oc, kk=kk, sh=sh: v.scalar_tensor_tensor(
                    out=cac_t[:, sh:T], in0=xmc_t[:, 0:T - sh],
                    scalar=cw_t[:, oc * 4 + kk:oc * 4 + kk + 1],
                    in1=cac_t[:, sh:T], op0=ALU.mult, op1=ALU.add),
                    reads=[xmc_b, cac_b], sreads=[cw_b], writes=[cac_b])
            p.op("scalar", lambda s, oc=oc: s.activation(
                out=xms_t[:, :], in_=cac_t[:, :], func=AF.Silu,
                bias=cb[0][:, oc:oc + 1], scale=1.0),
                reads=[cac_b], sreads=[cb[1]], writes=[xms_b])
            for tt in range(NQT):
                xd, xdb = ctx.ps[f"xd{tt}"]
                tok = p.op("tensor", lambda t, xd=xd, oc=oc, tt=tt: t.matmul(
                    out=xd[0:64, 0:512], lhsT=xpw_t[:, oc * 64:(oc + 1) * 64],
                    rhs=xms_t[:, tt * 512:(tt + 1) * 512],
                    start=(oc == 0), stop=(oc == 15)),
                    reads=[xpw_b, xms_b], writes=([xdb] if oc == 0 else []))
                xdb.writer = tok
        else:
            st_t, st_b = rst[oc % 2]
            for tt in range(NQT):
                psn = ("pma", "pmb")[tt % 2]
                mm_fm(ctx, psn, wt, wb, 128, NDC, 0, zn_t, zn_b, tt * 512, 512)
                pz, pzb = ctx.ps[psn]
                p.op("scalar", lambda s, pz=pz, oc=oc, tt=tt, st_t=st_t: s.activation(
                    out=st_t[:, tt * 512:(tt + 1) * 512], in_=pz[:, 0:512],
                    func=AF.Identity, bias=inb[0][:, oc:oc + 1], scale=1.0),
                    reads=[pzb], sreads=[inb[1]], writes=[st_b])
            p.dma("sync", o_res[:, (oc - 16) * T:(oc - 15) * T], st_t[:, :],
                  reads=[st_b])
    for tt in range(NQT):
        xd, xdb = ctx.ps[f"xd{tt}"]
        p.op("scalar", lambda s, xd=xd, tt=tt: s.activation(
            out=xdbl_sb[0][0:64, tt * 512:(tt + 1) * 512], in_=xd[0:64, 0:512],
            func=AF.Identity, bias=xpb_t[0:64, 0:1], scale=1.0),
            reads=[xdb], sreads=[xpb_b], writes=[xdbl_sb[1]])
    p.dma("sync", o_xdbl[:, :], xdbl_sb[0][0:64, :], reads=[xdbl_sb[1]])
    p.finalize(final_waits=p.barrier_tokens())
    return ctx


def _xpw_pad(ii):
    w = ii["L0_attn_xp_w"]
    wp = np.zeros((2048, 64), np.float32)
    wp[:, 0:16] = w[:, 0:16]
    wp[:, 32:48] = w[:, 16:32]
    return w_fm_view(wp)


def _rows_to_r(ctx, rows_t, rows_b):
    p = ctx.prog
    ept, epb = ctx.epsb()
    R0, R1v = rows_t[0:1, 0:T], rows_t[0:1, T:2 * T]
    p.op("vector", lambda v: v.tensor_scalar_mul(
        out=R0, in0=R0, scalar1=1.0 / D), reads=[rows_b], writes=[rows_b])
    p.op("vector", lambda v: v.tensor_scalar_mul(
        out=R1v, in0=R1v, scalar1=1.0 / D), reads=[rows_b], writes=[rows_b])
    p.op("vector", lambda v: v.scalar_tensor_tensor(
        out=R1v, in0=R0, scalar=-1.0, in1=R0, op0=ALU.mult, op1=ALU.mult),
        reads=[rows_b], writes=[rows_b]) if False else None
    if "lnscr" in ctx.sb:
        rs_t, rs_b = ctx.sb["lnscr"]
    elif "scr" in ctx.sb:
        rs_t, rs_b = ctx.sb["scr"]
    else:
        if "rsc" not in ctx.sb:
            ctx.sbuf("rsc", (1, T), F32)
        rs_t, rs_b = ctx.sb["rsc"]
    p.op("vector", lambda v, rs_t=rs_t: v.tensor_tensor(
        out=rs_t[0:1, 0:T], in0=R0, in1=R0, op=ALU.mult),
        reads=[rows_b], writes=[rs_b])
    p.op("vector", lambda v, rs_t=rs_t: v.tensor_tensor(
        out=R1v, in0=R1v, in1=rs_t[0:1, 0:T], op=ALU.subtract),
        reads=[rows_b, rs_b], writes=[rows_b])
    p.op("scalar", lambda s: s.activation(
        out=R1v, in_=R1v, func=AF.Sqrt, bias=ept[0:1, 0:1], scale=1.0),
        reads=[rows_b], sreads=[epb], writes=[rows_b])
    p.op("vector", lambda v: v.reciprocal(out=R1v, in_=R1v),
         reads=[rows_b], writes=[rows_b])
    p.op("vector", lambda v: v.tensor_tensor(
        out=R0, in0=R0, in1=R1v, op=ALU.mult), reads=[rows_b], writes=[rows_b])
    p.op("vector", lambda v: v.tensor_scalar_mul(
        out=R0, in0=R0, scalar1=-1.0), reads=[rows_b], writes=[rows_b])


def _bcast_rows(ctx, rows_t, rows_b, Rb, RMb, ones_t, ones_b, psname):
    p = ctx.prog
    npt, npb = ctx.ps[psname]
    for tgt, src_off in ((Rb, T), (RMb, 0)):
        for tt in range(NQT):
            p.op("tensor", lambda t, src_off=src_off, tt=tt: t.matmul(
                out=npt[:, 0:512], lhsT=ones_t[0:1, :],
                rhs=rows_t[0:1, src_off + tt * 512:src_off + (tt + 1) * 512],
                start=True, stop=True), reads=[ones_b, rows_b], writes=[npb])
            p.op("vector", lambda v, tgt=tgt, tt=tt: v.tensor_copy(
                out=tgt[0][:, tt * 512:(tt + 1) * 512], in_=npt[:, 0:512]),
                reads=[npb], writes=[tgt[1]])



def _emit_fm_lnstats(ctx, src_t, src_b, rows_t, rows_b, onesc, onescb, sqs,
                     ps_a, ps_b):
    p = ctx.prog
    onesc_t, onesc_b = onesc
    onescb_t, onescb_b = onescb
    pt, pb = ps_a
    pt2, pb2 = ps_b
    for tt in range(NQT):
        for mc in range(NDC):
            sl = src_t[:, mc * T + tt * 512:mc * T + (tt + 1) * 512]
            tok = p.op("tensor", lambda t, sl=sl, mc=mc, pt=pt: t.matmul(
                out=pt[0:1, 0:512], lhsT=onesc_t[:, 0:1], rhs=sl,
                start=(mc == 0), stop=(mc == NDC - 1)),
                reads=[src_b, onesc_b], writes=([pb] if mc == 0 else []))
            pb.writer = tok
            p.op("scalar", lambda s, sl=sl: s.activation(
                out=sqs[0][:, :], in_=sl, func=AF.Square),
                reads=[src_b], writes=[sqs[1]])
            tok = p.op("tensor", lambda t, mc=mc, pt2=pt2: t.matmul(
                out=pt2[0:1, 0:512], lhsT=onescb_t[:, 0:1], rhs=sqs[0][:, :],
                start=(mc == 0), stop=(mc == NDC - 1)),
                reads=[sqs[1], onescb_b], writes=([pb2] if mc == 0 else []))
            pb2.writer = tok
        p.op("vector", lambda v, tt=tt, pt=pt: v.tensor_copy(
            out=rows_t[0:1, tt * 512:(tt + 1) * 512], in_=pt[0:1, 0:512]),
            reads=[pb], writes=[rows_b])
        p.op("vector", lambda v, tt=tt, pt2=pt2: v.tensor_copy(
            out=rows_t[0:1, T + tt * 512:T + (tt + 1) * 512], in_=pt2[0:1, 0:512]),
            reads=[pb2], writes=[rows_b])


def build_scan_a():
    ctx = Ctx("scana")
    p = ctx.prog
    z_ap = ctx.dram_in("z", (128, NDC * T), F32, None)
    res_ap = ctx.dram_in("res", (128, 16 * T), BF16, None)
    xdblf_ap = ctx.dram_in("xdblf", (64, S), F32, None)
    xzbw_ap = ctx.dram_in("xzbw", (128, 3 * 16 * 6), F32, None)
    ow_ap = ctx.dram_in("ow", (128, 16 * D), F32,
                        lambda c, ii: w_fm_view(ii["L0_attn_out_w"]))
    ob = load_vec_fm(ctx, "ob", "obs", D, lambda c, ii: ii["L0_attn_out_b"])
    cw_ap = ctx.dram_in("convw", (128, 16 * 4), F32,
                        lambda c, ii: ii["L0_attn_conv_w"].reshape(16, 128, 4)
                        .transpose(1, 0, 2).reshape(128, 64))
    cb = load_vec_fm(ctx, "convb", "convbs", 2048, lambda c, ii: ii["L0_attn_conv_b"])
    xpw_ap = ctx.dram_in("xpw", (128, 16 * 64), F32, lambda c, ii: _xpw_pad(ii))
    xpb_ap = ctx.dram_in("xpb", (64, 1), F32,
                         lambda c, ii: np.concatenate(
                             [ii["L0_attn_xp_b"][0:16], np.zeros(16, np.float32),
                              ii["L0_attn_xp_b"][16:32], np.zeros(16, np.float32)]
                         ).reshape(64, 1))
    dt_ap = ctx.dram_in("dtv", (1, 16), F32,
                        lambda c, ii: np.ascontiguousarray(
                            ii["L0_attn_dt"][0:16]).reshape(1, 16))
    at_ap = ctx.dram_in("atm", (16, 16), F32,
                        lambda c, ii: np.ascontiguousarray(ii["L0_attn_A"].T))
    o_z2 = ctx.dram_out("o_z2", (128, NDC * T), F32)
    hsq_dram = ctx.nc.dram_tensor("hsq", [4, 16, T], F32, kind="Internal").ap()
    rred_dram = ctx.nc.dram_tensor("rredd", [16, D], F32, kind="Internal").ap()

    for nm in ["pma", "pmb", "psa", "psb", "pr", "pg", "pg2", "pg3"]:
        ctx.psum(nm, (128, 512), F32)

    ones_t, ones_b = ctx.sbuf("ones", (128, 128), F32)
    onesc_t, onesc_b = ctx.sbuf("onesc", (128, 1), F32)
    onescb_t, onescb_b = ctx.sbuf("onescb", (128, 1), BF16)
    p.op("vector", lambda v: v.memset(ones_t[:, :], 1.0), writes=[ones_b])
    p.op("vector", lambda v: v.memset(onesc_t[:, :], 1.0), writes=[onesc_b])
    p.op("vector", lambda v: v.memset(onescb_t[:, :], 1.0), writes=[onescb_b])
    z_t, z_b = ctx.sbuf("z", (128, NDC * T), F32)
    p.dma("sync", z_t[:, :], z_ap[:, :], writes=[z_b])
    xdbl_t, xdbl_b = ctx.sbuf("xdblf", (16, 2 * S), F32)
    p.dma("sync", xdbl_t[0:16, 0:S], xdblf_ap[0:16, :], writes=[xdbl_b])
    p.dma("sync", xdbl_t[0:16, S:2 * S], xdblf_ap[32:48, :], writes=[xdbl_b])
    cw_t, cw_b = ctx.sbuf("convw", (128, 64), F32)
    p.dma("sync", cw_t[:, :], cw_ap[:, :], writes=[cw_b])
    xpw_t, xpw_b = ctx.sbuf("xpw", (128, 16 * 64), BF16)
    p.dma("gdma", xpw_t[:, :], xpw_ap[:, :], writes=[xpw_b])
    xpb_t, xpb_b = ctx.sbuf("xpb", (64, 1), F32)
    p.dma("sync", xpb_t[0:64, :], xpb_ap[:, :], writes=[xpb_b])
    xzb_t, xzb_b = ctx.sbuf("xzbw", (128, 3 * 16 * 6), F32)
    p.dma("sync", xzb_t[:, :], xzbw_ap[:, :], writes=[xzb_b])

    pw_t, pw_b = ctx.sbuf("pwin", (128, 3), F32)
    pm_t, pm_b = ctx.sbuf("pms", (128, 16 * 3), BF16)
    for bd in range(3):
        pos = (bd + 1) * T
        for oc in range(16):
            w6 = xzb_t[:, bd * 96 + oc * 6:bd * 96 + oc * 6 + 6]
            p.op("vector", lambda v, oc=oc, w6=w6: v.tensor_scalar_mul(
                out=pw_t[:, :], in0=w6[:, 3:6], scalar1=cw_t[:, oc * 4 + 3:oc * 4 + 4]),
                reads=[xzb_b], sreads=[cw_b], writes=[pw_b])
            for kk in (2, 1, 0):
                p.op("vector", lambda v, oc=oc, kk=kk, w6=w6: v.scalar_tensor_tensor(
                    out=pw_t[:, :], in0=w6[:, kk:kk + 3],
                    scalar=cw_t[:, oc * 4 + kk:oc * 4 + kk + 1],
                    in1=pw_t[:, :], op0=ALU.mult, op1=ALU.add),
                    reads=[xzb_b, pw_b], sreads=[cw_b], writes=[pw_b])
            p.op("scalar", lambda s, oc=oc: s.activation(
                out=pm_t[:, oc * 3:(oc + 1) * 3], in_=pw_t[:, :], func=AF.Silu,
                bias=cb[0][:, oc:oc + 1], scale=1.0),
                reads=[pw_b], sreads=[cb[1]], writes=[pm_b])
        pp, ppb = ctx.ps["pg"]
        for oc in range(16):
            tok = p.op("tensor", lambda t, oc=oc, pp=pp: t.matmul(
                out=pp[0:64, 0:3], lhsT=xpw_t[:, oc * 64:(oc + 1) * 64],
                rhs=pm_t[:, oc * 3:(oc + 1) * 3],
                start=(oc == 0), stop=(oc == 15)),
                reads=[xpw_b, pm_b], writes=([ppb] if oc == 0 else []))
            ppb.writer = tok
        p.op("scalar", lambda s, pos=pos, pp=pp, ppb=ppb: s.activation(
            out=xdbl_t[0:16, pos:pos + 3], in_=pp[0:16, 0:3],
            func=AF.Identity, bias=xpb_t[0:16, 0:1], scale=1.0),
            reads=[ppb], sreads=[xpb_b], writes=[xdbl_b])
        p.op("scalar", lambda s, pos=pos, pp=pp, ppb=ppb: s.activation(
            out=xdbl_t[0:16, S + pos:S + pos + 3], in_=pp[32:48, 0:3],
            func=AF.Identity, bias=xpb_t[32:48, 0:1], scale=1.0),
            reads=[ppb], sreads=[xpb_b], writes=[xdbl_b])

    rr_row_t, rr_row_b = ctx.sbuf("rrrow", (1, D), F32)
    owg = [ctx.sbuf("owg0", (128, D), BF16)]
    pt, pb = ctx.ps["pr"]
    for g in range(16):
        wg_t, wg_b = owg[0]
        p.dma("gdma", wg_t[:, :], ow_ap[:, g * D:(g + 1) * D], writes=[wg_b])
        for half in range(2):
            p.op("tensor", lambda t, half=half, pt=pt, wg_t=wg_t: t.matmul(
                out=pt[0:1, 0:512], lhsT=onescb_t[:, 0:1],
                rhs=wg_t[:, half * 512:(half + 1) * 512],
                start=True, stop=True), reads=[onescb_b, wg_b], writes=[pb])
            p.op("vector", lambda v, half=half, pt=pt: v.tensor_copy(
                out=rr_row_t[0:1, half * 512:(half + 1) * 512], in_=pt[0:1, 0:512]),
                reads=[pb], writes=[rr_row_b])
        p.dma("sync", rred_dram[g:g + 1, :], rr_row_t[0:1, :], reads=[rr_row_b])
    rred_t, rred_b = ctx.sbuf("rred", (16, D), F32)
    p.dma("sync", rred_t[0:16, :], rred_dram[:, :],
          deps=[t for t in [p.dma_out[s] for s in p.dma_pool["sync"]] if t],
          writes=[rred_b])

    resst = [ctx.sbuf(f"resst{k}", (128, 512), BF16) for k in range(2)]
    owm = [ctx.sbuf(f"owmm{k}", (128, NDC * 128), BF16) for k in range(2)]
    MCPS = ["pma", "pmb", "psa", "psb", "pr", "pg", "pg2", "pg3"]
    for tt in range(NQT):
        for kc in range(16):
            rs_t, rs_b = resst[kc % 2]
            p.dma("sync", rs_t[:, :],
                  res_ap[:, kc * T + tt * 512:kc * T + (tt + 1) * 512],
                  writes=[rs_b])
            wt, wb = owm[kc % 2]
            p.dma("gdma", wt[:, :], ow_ap[:, kc * D:(kc + 1) * D], writes=[wb])
            for mc in range(NDC):
                pz, pzb = ctx.ps[MCPS[mc]]
                tok = p.op("tensor", lambda t, kc=kc, mc=mc, pz=pz, wt=wt,
                           rs_t=rs_t: t.matmul(
                    out=pz[:, 0:512], lhsT=wt[:, mc * 128:(mc + 1) * 128],
                    rhs=rs_t[:, :], start=(kc == 0), stop=(kc == 15)),
                    reads=[wb, rs_b], writes=([pzb] if kc == 0 else []))
                pzb.writer = tok
        for mc in range(NDC):
            pz, pzb = ctx.ps[MCPS[mc]]
            p.op("vector", lambda v, mc=mc, tt=tt, pz=pz, pzb=pzb: v.scalar_tensor_tensor(
                out=z_t[:, mc * T + tt * 512:mc * T + (tt + 1) * 512], in0=pz[:, 0:512],
                scalar=ob[0][:, mc:mc + 1],
                in1=z_t[:, mc * T + tt * 512:mc * T + (tt + 1) * 512],
                op0=ALU.add, op1=ALU.add),
                reads=[pzb, z_b], sreads=[ob[1]], writes=[z_b])

    dts_t, dts_b = ctx.sbuf("dts", (1, 16), F32)
    p.dma("sync", dts_t[0:1, :], dt_ap[:, :], writes=[dts_b])
    p.op("scalar", lambda s: s.activation(
        out=dts_t[0:1, :], in_=dts_t[0:1, :], func=AF.Exp, scale=1.0),
        reads=[dts_b], writes=[dts_b])
    at_t, at_b = ctx.sbuf("atm", (16, 16), F32)
    p.dma("sync", at_t[0:16, :], at_ap[:, :], writes=[at_b])
    db_t, db_b = ctx.sbuf("dbm", (16, 16), F32)
    pt2, pb2 = ctx.ps["pr"]
    p.op("tensor", lambda t, pt2=pt2: t.matmul(
        out=pt2[0:16, 0:16], lhsT=ones_t[0:1, 0:16], rhs=dts_t[0:1, :],
        start=True, stop=True), reads=[ones_b, dts_b], writes=[pb2])
    p.op("vector", lambda v, pt2=pt2: v.tensor_copy(
        out=db_t[0:16, :], in_=pt2[0:16, 0:16]), reads=[pb2], writes=[db_b])
    ats_t, ats_b = ctx.sbuf("ats", (16, 16 * 16), F32)
    for r in range(16):
        p.op("scalar", lambda s, r=r: s.activation(
            out=ats_t[0:16, r * 16:(r + 1) * 16], in_=at_t[0:16, :], func=AF.Exp,
            scale=db_t[0:16, r:r + 1]),
            reads=[at_b], sreads=[db_b], writes=[ats_b])

    hs_t, hs_b = ctx.sbuf("hs", (16, S), F32)
    hz_t, hz_b = ctx.sbuf("hz", (16, 1), F32)
    p.op("vector", lambda v: v.memset(hz_t[0:16, :], 0.0), writes=[hz_b])
    ttmp = [ctx.sbuf(f"ttmp{k}", (16, 1), F32) for k in range(2)]
    wcol = [ctx.sbuf(f"wcol{k}", (16, 1), F32) for k in range(2)]
    pscan = [ctx.ps["psa"], ctx.ps["psb"]]
    for t in range(S):
        r = t % 16
        hprev_col = hz_t[0:16, 0:1] if t == 0 else hs_t[0:16, t - 1:t]
        hprev_b = hz_b if t == 0 else hs_b
        psn, psb_ = pscan[t % 2]
        tm_t, tm_b = ttmp[t % 2]
        wc_t, wc_b = wcol[t % 2]
        p.op("vector", lambda v, t=t, hprev_col=hprev_col, wc_t=wc_t: v.tensor_scalar(
            out=wc_t[0:16, :], in0=hprev_col,
            scalar1=xdbl_t[0:16, t:t + 1], scalar2=xdbl_t[0:16, S + t:S + t + 1],
            op0=ALU.mult, op1=ALU.add),
            reads=[hprev_b], sreads=[xdbl_b], writes=[wc_b])
        p.op("tensor", lambda t_, r=r, hprev_col=hprev_col, psn=psn: t_.matmul(
            out=psn[0:16, 0:1], lhsT=ats_t[0:16, r * 16:(r + 1) * 16],
            rhs=hprev_col, start=True, stop=True),
            reads=[ats_b, hprev_b], writes=[psb_])
        p.op("scalar", lambda s, psn=psn, tm_t=tm_t: s.activation(
            out=tm_t[0:16, :], in_=psn[0:16, 0:1], func=AF.Tanh, scale=1.0),
            reads=[psb_], writes=[tm_b])
        p.op("scalar", lambda s, t=t, tm_t=tm_t, wc_t=wc_t: s.activation(
            out=hs_t[0:16, t:t + 1], in_=tm_t[0:16, 0:1], func=AF.Identity,
            bias=wc_t[0:16, 0:1], scale=1.0),
            wreads=[tm_b], sreads=[wc_b], writes=[hs_b])

    for qd in range(4):
        p.dma("sync", hsq_dram[qd, :, :], hs_t[0:16, qd * T:(qd + 1) * T],
              reads=[hs_b])
    hso_t, hso_b = ctx.sbuf("hso", (16, T), F32)

    def emit_dyn(eng):
        pid = eng.compute_val(ctx.nc.partition_id([mybir.EngineType.Pool]))
        q = pid % 4
        return eng.dma_start(out=hso_t[0:16, :], in_=hsq_dram[bass.ts(q, 1), :, :])

    p.op("gdma" if False else "pool", emit_dyn, deps=[
        t for t in [p.dma_out[s] for s in p.dma_pool["sync"]] if t],
        writes=[hso_b]) if False else None
    dlist = [t for t in [p.dma_out[s] for s in p.dma_pool["sync"]] if t]
    sem = p.dma_pool["gdma"][p.dma_rr["gdma"] % len(p.dma_pool["gdma"])]
    p.dma_rr["gdma"] += 1
    prev = p.dma_out[sem]
    if prev is not None:
        dlist.append(prev)
    tokd = (sem, p.counts[sem] + 16, "gdma")
    p.counts[sem] += 16
    p._record("gpsimd", sem, emit_dyn, [(t, False) for t in dlist], 16, tokd)
    p.dma_out[sem] = tokd
    hso_b.writer = tokd

    for mc in range(NDC):
        for tt in range(NQT):
            psn = ("pma", "pmb")[tt % 2]
            pz, pzb = ctx.ps[psn]
            p.op("tensor", lambda t, mc=mc, tt=tt, pz=pz: t.matmul(
                out=pz[:, 0:512], lhsT=rred_t[0:16, mc * 128:(mc + 1) * 128],
                rhs=hso_t[0:16, tt * 512:(tt + 1) * 512], start=True, stop=True),
                reads=[rred_b, hso_b], writes=[pzb])
            p.op("vector", lambda v, mc=mc, tt=tt, pz=pz, pzb=pzb: v.tensor_tensor(
                out=z_t[:, mc * T + tt * 512:mc * T + (tt + 1) * 512], in0=pz[:, 0:512],
                in1=z_t[:, mc * T + tt * 512:mc * T + (tt + 1) * 512], op=ALU.add),
                reads=[pzb, z_b], writes=[z_b])

    for mc in range(NDC):
        p.dma("sync", o_z2[:, mc * T:(mc + 1) * T], z_t[:, mc * T:(mc + 1) * T],
              reads=[z_b])

    p.finalize(final_waits=p.barrier_tokens())
    return ctx



def build_mlp_assemble():
    ctx = Ctx("mlpasm")
    p = ctx.prog
    z2_ap = ctx.dram_in("z2", (128, NDC * T), F32, None)
    ffn_ap = ctx.dram_in("ffn", (128, NDC * T), F32, None)
    x_ap = ctx.dram_in("x", (128, NTT * D), F32, x_source_from_key("L0_attn_x"))
    fc1_ap = ctx.dram_in("fc1", (128, NDC * 4096), F32,
                         lambda c, ii: w_fm_view(ii["L0_attn_fc1_w"]))
    fc1b = load_vec_fm(ctx, "fc1b", "fc1bs", 4096, lambda c, ii: ii["L0_attn_fc1_b"])
    fc2_ap = ctx.dram_in("fc2", (128, 32 * D), F32,
                         lambda c, ii: w_fm_view(ii["L0_attn_fc2_w"]))
    fc2b_ap = ctx.dram_in("fc2b", (1, D), F32,
                          lambda c, ii: ii["L0_attn_fc2_b"].reshape(1, D))
    g3 = load_vec_fm(ctx, "g3", "g3s", D, lambda c, ii: ii["L0_attn_ln3_g"])
    b3 = load_vec_fm(ctx, "b3", "b3s", D, lambda c, ii: ii["L0_attn_ln3_b"])
    idf_ap = ctx.dram_in("idf", (128, 128), F32, lambda c, ii: const_identity_f32())
    o_x1 = ctx.dram_out("o_x1", (128, NTT * D), F32)

    for nm in ["pma", "pmb", "tpf", "pr", "ph0", "ph1", "ph2", "ph3"]:
        ctx.psum(nm, (128, 512), F32)

    ones_t, ones_b = ctx.sbuf("ones", (128, 128), F32)
    onesc_t, onesc_b = ctx.sbuf("onesc", (128, 1), F32)
    onescb_t, onescb_b = ctx.sbuf("onescb", (128, 1), BF16)
    p.op("vector", lambda v: v.memset(ones_t[:, :], 1.0), writes=[ones_b])
    p.op("vector", lambda v: v.memset(onesc_t[:, :], 1.0), writes=[onesc_b])
    p.op("vector", lambda v: v.memset(onescb_t[:, :], 1.0), writes=[onescb_b])
    z_t, z_b = ctx.sbuf("z2", (128, NDC * T), F32)
    p.dma("sync", z_t[:, :], z2_ap[:, :], writes=[z_b])
    idf_t, idf_b = ctx.sbuf("idf", (128, 128), F32)
    p.dma("sync", idf_t[:, :], idf_ap[:, :], writes=[idf_b])
    fc2b_t, fc2b_b = ctx.sbuf("fc2brow", (1, D), F32)
    p.dma("sync", fc2b_t[0:1, :], fc2b_ap[:, :], writes=[fc2b_b])
    brep_t, brep_b = ctx.sbuf("brep", (128, D), F32)
    pt5, pb5 = ctx.ps["pr"]
    for half in range(2):
        p.op("tensor", lambda t, half=half, pt5=pt5: t.matmul(
            out=pt5[:, 0:512], lhsT=ones_t[0:1, :],
            rhs=fc2b_t[0:1, half * 512:(half + 1) * 512], start=True, stop=True),
            reads=[ones_b, fc2b_b], writes=[pb5])
        p.op("vector", lambda v, half=half, pt5=pt5: v.tensor_copy(
            out=brep_t[:, half * 512:(half + 1) * 512], in_=pt5[:, 0:512]),
            reads=[pb5], writes=[brep_b])

    sqs = ctx.sbuf("sqs", (128, 512), BF16)
    rows_t, rows_b = ctx.sbuf("rows", (1, 2 * T), F32)
    _emit_fm_lnstats(ctx, z_t, z_b, rows_t, rows_b, (onesc_t, onesc_b),
                     (onescb_t, onescb_b), sqs, ctx.ps["pr"], ctx.ps["ph0"])
    _rows_to_r(ctx, rows_t, rows_b)
    Rb = ctx.sbuf("Rb", (128, T), F32)
    RMb = ctx.sbuf("RMb", (128, T), F32)
    _bcast_rows(ctx, rows_t, rows_b, Rb, RMb, ones_t, ones_b, "pr")
    scr_t, scr_b = ctx.sbuf("scr", (128, 512), F32)

    fc1w = [ctx.sbuf(f"fc1w{k}", (128, NDC * 128), BF16) for k in range(2)]
    fc2s = [ctx.sbuf(f"fc2s{k}", (128, 512), BF16) for k in range(2)]
    g_tt_t, g_tt_b = ctx.sbuf("g_tt", (128, 32 * 512), BF16)
    h3_t, h3_b = ctx.sbuf("h3tt", (128, NDC * 512), BF16)
    xst = [ctx.sbuf(f"xst{k}", (128, D), F32) for k in range(4)]
    x1t = [ctx.sbuf(f"x1t{k}", (128, D), F32) for k in range(4)]
    ffs = [ctx.sbuf(f"ffs{k}", (128, 128), F32) for k in range(2)]
    for tt in range(NQT):
        for mc in range(NDC):
            p.op("vector", lambda v, mc=mc, tt=tt: v.tensor_tensor(
                out=scr_t[:, :], in0=z_t[:, mc * T + tt * 512:mc * T + (tt + 1) * 512],
                in1=Rb[0][:, tt * 512:(tt + 1) * 512], op=ALU.mult),
                reads=[z_b, Rb[1]], writes=[scr_b])
            p.op("vector", lambda v, tt=tt: v.tensor_tensor(
                out=scr_t[:, :], in0=scr_t[:, :],
                in1=RMb[0][:, tt * 512:(tt + 1) * 512], op=ALU.add),
                reads=[scr_b, RMb[1]], writes=[scr_b])
            p.op("vector", lambda v, mc=mc: v.tensor_scalar(
                out=h3_t[:, mc * 512:(mc + 1) * 512], in0=scr_t[:, :],
                scalar1=g3[0][:, mc:mc + 1], scalar2=b3[0][:, mc:mc + 1],
                op0=ALU.mult, op1=ALU.add),
                reads=[scr_b], sreads=[g3[1], b3[1]], writes=[h3_b])
        for kc in range(32):
            wt, wb = fc1w[kc % 2]
            p.dma("gdma", wt[:, :].rearrange("p (z n) -> p z n", z=NDC),
                  fc1_ap.rearrange("p (z n) -> p z n", z=NDC)[:, :, kc * 128:kc * 128 + 128],
                  writes=[wb])
            psn = ("pma", "pmb")[kc % 2]
            pz, pzb = ctx.ps[psn]
            for zc in range(NDC):
                tok = p.op("tensor", lambda t, zc=zc, pz=pz, wt=wt: t.matmul(
                    out=pz[:, 0:512], lhsT=wt[:, zc * 128:(zc + 1) * 128],
                    rhs=h3_t[:, zc * 512:(zc + 1) * 512],
                    start=(zc == 0), stop=(zc == NDC - 1)),
                    reads=[wb, h3_b], writes=([pzb] if zc == 0 else []))
                pzb.writer = tok
            p.op("scalar", lambda s, pz=pz, kc=kc: s.activation(
                out=g_tt_t[:, kc * 512:(kc + 1) * 512], in_=pz[:, 0:512],
                func=AF.Gelu_apprx_tanh, bias=fc1b[0][:, kc:kc + 1], scale=1.0),
                reads=[pzb], sreads=[fc1b[1]], writes=[g_tt_b])
        for half in range(2):
            for kc in range(32):
                ft, fb = fc2s[kc % 2]
                p.dma("gdma", ft[:, :],
                      fc2_ap[:, kc * D + half * 512:kc * D + (half + 1) * 512],
                      writes=[fb])
                for st in range(4):
                    pz, pzb = ctx.ps[f"ph{st}"]
                    tok = p.op("tensor", lambda t, kc=kc, st=st, pz=pz, ft=ft: t.matmul(
                        out=pz[:, 0:512],
                        lhsT=g_tt_t[:, kc * 512 + st * 128:kc * 512 + st * 128 + 128],
                        rhs=ft[:, :], start=(kc == 0), stop=(kc == 31)),
                        reads=[g_tt_b, fb], writes=([pzb] if kc == 0 else []))
                    pzb.writer = tok
            for st in range(4):
                i_tile = tt * 4 + st
                x1_t, x1_b = x1t[st]
                pz, pzb = ctx.ps[f"ph{st}"]
                if half == 0:
                    xs_t, xs_b = xst[st]
                    p.dma("sync", xs_t[:, :], x_ap[:, i_tile * D:(i_tile + 1) * D],
                          writes=[xs_b])
                    p.op("vector", lambda v, pz=pz, pzb=pzb, x1_t=x1_t,
                         xs_t=xs_t, xs_b=xs_b: v.tensor_tensor(
                             out=x1_t[:, 0:512], in0=pz[:, 0:512],
                             in1=xs_t[:, 0:512], op=ALU.add),
                         reads=[pzb, xs_b], writes=[x1_b])
                else:
                    xs_t, xs_b = xst[st]
                    p.op("vector", lambda v, pz=pz, pzb=pzb, x1_t=x1_t,
                         xs_t=xs_t, xs_b=xs_b: v.tensor_tensor(
                             out=x1_t[:, 512:1024], in0=pz[:, 0:512],
                             in1=xs_t[:, 512:1024], op=ALU.add),
                         reads=[pzb, xs_b], writes=[x1_b])
        for st in range(4):
            i_tile = tt * 4 + st
            x1_t, x1_b = x1t[st]
            p.op("vector", lambda v, x1_t=x1_t, x1_b=x1_b: v.tensor_tensor(
                out=x1_t[:, :], in0=x1_t[:, :], in1=brep_t[:, :], op=ALU.add),
                reads=[x1_b, brep_b], writes=[x1_b])
            for dc in range(NDC):
                ptp, ptpb = ctx.ps["tpf"]
                p.op("tensor", lambda t, dc=dc, i_tile=i_tile, ptp=ptp: t.transpose(
                    out=ptp[:, 0:128],
                    in_=z_t[:, dc * T + i_tile * 128:dc * T + i_tile * 128 + 128],
                    identity=idf_t[:, :]),
                    reads=[z_b, idf_b], writes=[ptpb])
                p.op("vector", lambda v, dc=dc, ptp=ptp, ptpb=ptpb, x1_t=x1_t,
                     x1_b=x1_b: v.tensor_tensor(
                         out=x1_t[:, dc * 128:(dc + 1) * 128], in0=ptp[:, 0:128],
                         in1=x1_t[:, dc * 128:(dc + 1) * 128], op=ALU.add),
                     reads=[ptpb, x1_b], writes=[x1_b])
                ff_t, ff_b = ffs[dc % 2]
                p.dma("sync", ff_t[:, :],
                      ffn_ap[:, dc * T + i_tile * 128:dc * T + i_tile * 128 + 128],
                      writes=[ff_b])
                ptp2, ptpb2 = ctx.ps["pr"]
                p.op("tensor", lambda t, ptp2=ptp2, ff_t=ff_t, ff_b=ff_b: t.transpose(
                    out=ptp2[:, 0:128], in_=ff_t[:, :], identity=idf_t[:, :]),
                    reads=[ff_b, idf_b], writes=[ptpb2])
                p.op("vector", lambda v, dc=dc, ptp2=ptp2, ptpb2=ptpb2, x1_t=x1_t,
                     x1_b=x1_b: v.tensor_tensor(
                         out=x1_t[:, dc * 128:(dc + 1) * 128], in0=ptp2[:, 0:128],
                         in1=x1_t[:, dc * 128:(dc + 1) * 128], op=ALU.add),
                     reads=[ptpb2, x1_b], writes=[x1_b])
            p.dma("sync", o_x1[:, i_tile * D:(i_tile + 1) * D], x1_t[:, :],
                  reads=[x1_b])

    p.finalize(final_waits=p.barrier_tokens())
    return ctx


def l0_host_chain(inputs, pre, attn_res):
    mf_extra = {"z": [attn_res[c]["o_z"] for c in range(NCORE)]}
    return mf_extra


def scan_extra(inputs, pre, attn_res, mf_res):
    xdblf = {}
    xzbw = {}
    for b in range(2):
        cores = [b * 4 + i for i in range(4)]
        xdblf[b] = np.concatenate([mf_res[c]["o_xdbl"] for c in cores], axis=1)
        w = np.zeros((128, 3 * 16 * 6), np.float32)
        for bd in range(3):
            prevc, nextc = cores[bd], cores[bd + 1]
            for oc in range(16):
                w[:, bd * 96 + oc * 6:bd * 96 + oc * 6 + 3] = \
                    mf_res[prevc]["o_xzb"][:, oc * 6 + 3:oc * 6 + 6]
                w[:, bd * 96 + oc * 6 + 3:bd * 96 + oc * 6 + 6] = \
                    mf_res[nextc]["o_xzb"][:, oc * 6:oc * 6 + 3]
        xzbw[b] = w
    return {
        "z": [attn_res[c]["o_z"] for c in range(NCORE)],
        "res": [mf_res[c]["o_res"] for c in range(NCORE)],
        "xdblf": [xdblf[c // 4] for c in range(NCORE)],
        "xzbw": [xzbw[c // 4] for c in range(NCORE)],
    }



def emit_router(ctx, xnfm, wg_sb, comb_t, comb_b, psname, renorm, sum_col=None):
    p = ctx.prog
    xnfm_t, xnfm_b = xnfm
    wg_t, wg_b = wg_sb
    pt, pb = ctx.ps[psname]
    st_t, st_b = ctx.sbuf("rt_scr", (128, 16), F32)
    for it in range(NTT):
        for kc in range(NDC):
            tok = p.op("tensor", lambda t, it=it, kc=kc, pt=pt: t.matmul(
                out=pt[:, 0:8], lhsT=xnfm_t[:, kc * T + it * 128:kc * T + it * 128 + 128],
                rhs=wg_t[:, kc * 8:(kc + 1) * 8], start=(kc == 0), stop=(kc == NDC - 1)),
                reads=[xnfm_b, wg_b], writes=([pb] if kc == 0 else []))
            pb.writer = tok
        c0 = it * 8
        sl = comb_t[:, c0:c0 + 8]
        p.op("vector", lambda v, pt=pt: v.tensor_reduce(
            out=st_t[:, 0:1], in_=pt[:, 0:8], axis=AX.X, op=ALU.max),
            reads=[pb], writes=[st_b])
        p.op("vector", lambda v: v.tensor_scalar_mul(
            out=st_t[:, 1:2], in0=st_t[:, 0:1], scalar1=-1.0),
            reads=[st_b], writes=[st_b])
        p.op("scalar", lambda s, sl=sl, pt=pt: s.activation(
            out=sl, in_=pt[:, 0:8], func=AF.Exp, bias=st_t[:, 1:2], scale=1.0),
            reads=[pb], sreads=[st_b], writes=[comb_b])
        p.op("vector", lambda v, sl=sl: v.tensor_reduce(
            out=st_t[:, 2:3], in_=sl, axis=AX.X, op=ALU.add),
            reads=[comb_b], writes=[st_b])
        p.op("vector", lambda v: v.reciprocal(
            out=st_t[:, 3:4], in_=st_t[:, 2:3]), reads=[st_b], writes=[st_b])
        p.op("vector", lambda v, sl=sl: v.tensor_scalar_mul(
            out=sl, in0=sl, scalar1=st_t[:, 3:4]),
            reads=[comb_b], sreads=[st_b], writes=[comb_b])
        p.op("vector", lambda v, sl=sl: v.tensor_reduce(
            out=st_t[:, 4:5], in_=sl, axis=AX.X, op=ALU.max),
            reads=[comb_b], writes=[st_b])
        p.op("vector", lambda v, sl=sl: v.tensor_single_scalar(
            out=st_t[:, 8:16], in_=sl, scalar=st_t[:, 4:5], op=ALU.is_ge),
            reads=[comb_b], sreads=[st_b], writes=[st_b])
        p.op("vector", lambda v, sl=sl: v.scalar_tensor_tensor(
            out=st_t[:, 8:16], in0=st_t[:, 8:16], scalar=-1e30, in1=sl,
            op0=ALU.mult, op1=ALU.add), reads=[st_b, comb_b], writes=[st_b])
        p.op("vector", lambda v: v.tensor_reduce(
            out=st_t[:, 5:6], in_=st_t[:, 8:16], axis=AX.X, op=ALU.max),
            reads=[st_b], writes=[st_b])
        p.op("vector", lambda v, sl=sl: v.tensor_single_scalar(
            out=st_t[:, 8:16], in_=sl, scalar=st_t[:, 5:6], op=ALU.is_ge),
            reads=[comb_b], sreads=[st_b], writes=[st_b])
        p.op("vector", lambda v, sl=sl: v.tensor_tensor(
            out=sl, in0=sl, in1=st_t[:, 8:16], op=ALU.mult),
            reads=[comb_b, st_b], writes=[comb_b])
        if renorm:
            p.op("vector", lambda v: v.tensor_tensor(
                out=st_t[:, 6:7], in0=st_t[:, 4:5], in1=st_t[:, 5:6], op=ALU.add),
                reads=[st_b], writes=[st_b])
            p.op("vector", lambda v: v.tensor_scalar_add(
                out=st_t[:, 6:7], in0=st_t[:, 6:7], scalar1=1e-9),
                reads=[st_b], writes=[st_b])
            p.op("vector", lambda v: v.reciprocal(
                out=st_t[:, 7:8], in_=st_t[:, 6:7]), reads=[st_b], writes=[st_b])
            p.op("vector", lambda v, sl=sl: v.tensor_scalar_mul(
                out=sl, in0=sl, scalar1=st_t[:, 7:8]),
                reads=[comb_b], sreads=[st_b], writes=[comb_b])
        if sum_col is not None:
            p.op("vector", lambda v, sl=sl, it=it: v.tensor_reduce(
                out=sum_col[0][:, it:it + 1], in_=sl, axis=AX.X, op=ALU.add),
                reads=[comb_b], writes=[sum_col[1]])


def build_ffn1():
    ctx = Ctx("ffn1")
    p = ctx.prog
    xn_ap = ctx.dram_in("xn", (128, NDC * T), BF16, None)
    x_ap = ctx.dram_in("x", (128, NTT * D), F32, None)
    b1a_ap = ctx.dram_in("b1a", (128, NDC * T), F32, None)
    wg_ap = ctx.dram_in("wg", (128, NDC * 8), F32,
                        lambda c, ii: w_fm_view(np.ascontiguousarray(ii["L1_ffn_Wg"].T)))
    w1_ap = [ctx.dram_in(f"w1_{e}", (128, NDC * 4096), F32,
                         lambda c, ii, e=e: w_fm_view(
                             np.ascontiguousarray(ii["L1_ffn_W1"][e].T)))
             for e in range(8)]
    b1_ap = ctx.dram_in("b1e", (128, 8 * 32), F32,
                        lambda c, ii: np.concatenate(
                            [vec_fm_view(ii["L1_ffn_b1"][e]) for e in range(8)], axis=1))
    wo_ap = ctx.dram_in("woT", (128, 16 * D), F32,
                        lambda c, ii: w_fm_view(np.ascontiguousarray(ii["L1_ffn_Wo"].T)))
    bo_ap = ctx.dram_in("bo1", (1, D), F32, lambda c, ii: ii["L1_ffn_bo"].reshape(1, D))
    idf_ap = ctx.dram_in("idf", (128, 128), F32, lambda c, ii: const_identity_f32())
    o_x2 = ctx.dram_out("o_x2", (128, NTT * D), F32)

    for nm in ["pma", "pmb", "pr", "pg", "tpf"]:
        ctx.psum(nm, (128, 512), F32)

    ones_t, ones_b = ctx.sbuf("ones", (128, 128), F32)
    p.op("vector", lambda v: v.memset(ones_t[:, :], 1.0), writes=[ones_b])
    idf_t, idf_b = ctx.sbuf("idf", (128, 128), F32)
    p.dma("sync", idf_t[:, :], idf_ap[:, :], writes=[idf_b])
    xn_t, xn_b = ctx.sbuf("xn", (128, NDC * T), BF16)
    p.dma("sync", xn_t[:, :], xn_ap[:, :], writes=[xn_b])
    x_t, x_b = ctx.sbuf("x", (128, NTT * D), F32)
    p.dma("sync", x_t[:, :], x_ap[:, :], writes=[x_b])
    wg_t, wg_b = ctx.sbuf("wg", (128, NDC * 8), BF16)
    p.dma("gdma", wg_t[:, :], wg_ap[:, :], writes=[wg_b])
    b1_t, b1_b = ctx.sbuf("b1e", (128, 8 * 32), F32)
    p.dma("sync", b1_t[:, :], b1_ap[:, :], writes=[b1_b])
    wo_t, wo_b = ctx.sbuf("woT", (128, 16 * D), BF16)
    p.dma("gdma", wo_t[:, :], wo_ap[:, :], writes=[wo_b])
    bo_t, bo_b = ctx.sbuf("bo1r", (1, D), F32)
    p.dma("sync", bo_t[0:1, :], bo_ap[:, :], writes=[bo_b])
    borep_t, borep_b = ctx.sbuf("borep", (128, D), F32)
    pt5, pb5 = ctx.ps["pr"]
    for half in range(2):
        p.op("tensor", lambda t, half=half, pt5=pt5: t.matmul(
            out=pt5[:, 0:512], lhsT=ones_t[0:1, :],
            rhs=bo_t[0:1, half * 512:(half + 1) * 512], start=True, stop=True),
            reads=[ones_b, bo_b], writes=[pb5])
        p.op("vector", lambda v, half=half, pt5=pt5: v.tensor_copy(
            out=borep_t[:, half * 512:(half + 1) * 512], in_=pt5[:, 0:512]),
            reads=[pb5], writes=[borep_b])

    comb_t, comb_b = ctx.sbuf("comb", (128, NTT * 8), F32)
    csum = ctx.sbuf("csum", (128, NTT), F32)
    emit_router(ctx, (xn_t, xn_b), (wg_t, wg_b), comb_t, comb_b, "pr",
                renorm=True, sum_col=csum)

    ge_t, ge_b = ctx.sbuf("ge", (128, 16 * 1024), BF16)
    sgb_t, sgb_b = ctx.sbuf("sgb", (128, 1024), BF16)
    wmc = [ctx.sbuf(f"wmc{k}", (128, NDC * 128), BF16) for k in range(2)]
    TH = 1024
    for e in range(8):
        for th in range(2):
            for j in range(16):
                for which, oc in ((0, j + 16), (1, j)):
                    wt, wb = wmc[oc % 2]
                    p.dma("gdma", wt[:, :].rearrange("p (z n) -> p z n", z=NDC),
                          w1_ap[e].rearrange("p (z n) -> p z n", z=NDC)[
                              :, :, oc * 128:oc * 128 + 128],
                          writes=[wb])
                    for tq in range(2):
                        col0 = th * TH + tq * 512
                        psn = ("pma", "pmb", "pr", "pg")[(oc * 2 + tq) % 4]
                        mm_fm(ctx, psn, wt, wb, 128, NDC, 0, xn_t, xn_b, col0, 512)
                        pz, pzb = ctx.ps[psn]
                        if which == 0:
                            p.op("scalar", lambda s, pz=pz, e=e, oc=oc, tq=tq:
                                 s.activation(
                                     out=sgb_t[:, tq * 512:(tq + 1) * 512],
                                     in_=pz[:, 0:512], func=AF.Sigmoid,
                                     bias=b1_t[:, e * 32 + oc:e * 32 + oc + 1],
                                     scale=1.0),
                                 reads=[pzb], sreads=[b1_b], writes=[sgb_b])
                        else:
                            p.op("vector", lambda v, pz=pz, pzb=pzb, e=e, oc=oc,
                                 j=j, tq=tq: v.scalar_tensor_tensor(
                                     out=ge_t[:, j * TH + tq * 512:j * TH + (tq + 1) * 512],
                                     in0=pz[:, 0:512],
                                     scalar=b1_t[:, e * 32 + oc:e * 32 + oc + 1],
                                     in1=sgb_t[:, tq * 512:(tq + 1) * 512],
                                     op0=ALU.add, op1=ALU.mult),
                                 reads=[pzb, sgb_b], sreads=[b1_b], writes=[ge_b])
            for it8 in range(8):
                it = th * 8 + it8
                for half in range(2):
                    psn = ("pma", "pmb", "pr", "pg")[(it8 * 2 + half) % 4]
                    pz, pzb = ctx.ps[psn]
                    for hc in range(16):
                        tok = p.op("tensor", lambda t, hc=hc, it8=it8, half=half,
                                   pz=pz: t.matmul(
                            out=pz[:, 0:512],
                            lhsT=ge_t[:, hc * TH + it8 * 128:hc * TH + it8 * 128 + 128],
                            rhs=wo_t[:, hc * D + half * 512:hc * D + (half + 1) * 512],
                            start=(hc == 0), stop=(hc == 15)),
                            reads=[ge_b, wo_b], writes=([pzb] if hc == 0 else []))
                        pzb.writer = tok
                    p.op("vector", lambda v, it=it, e=e, half=half, pz=pz, pzb=pzb:
                         v.scalar_tensor_tensor(
                             out=x_t[:, it * D + half * 512:it * D + (half + 1) * 512],
                             in0=pz[:, 0:512],
                             scalar=comb_t[:, it * 8 + e:it * 8 + e + 1],
                             in1=x_t[:, it * D + half * 512:it * D + (half + 1) * 512],
                             op0=ALU.mult, op1=ALU.add),
                         reads=[pzb, x_b], sreads=[comb_b], writes=[x_b])

    b1ast = [ctx.sbuf(f"b1ast{k}", (128, 128), F32) for k in range(2)]
    for it in range(NTT):
        p.op("vector", lambda v, it=it: v.scalar_tensor_tensor(
            out=x_t[:, it * D:(it + 1) * D], in0=borep_t[:, :],
            scalar=csum[0][:, it:it + 1], in1=x_t[:, it * D:(it + 1) * D],
            op0=ALU.mult, op1=ALU.add),
            reads=[borep_b, x_b], sreads=[csum[1]], writes=[x_b])
        for dc in range(NDC):
            bs_t, bs_b = b1ast[dc % 2]
            p.dma("sync", bs_t[:, :],
                  b1a_ap[:, dc * T + it * 128:dc * T + it * 128 + 128], writes=[bs_b])
            ptp, ptpb = ctx.ps["tpf"]
            p.op("tensor", lambda t, ptp=ptp, bs_t=bs_t, bs_b=bs_b: t.transpose(
                out=ptp[:, 0:128], in_=bs_t[:, :], identity=idf_t[:, :]),
                reads=[bs_b, idf_b], writes=[ptpb])
            p.op("vector", lambda v, it=it, dc=dc, ptp=ptp, ptpb=ptpb: v.tensor_tensor(
                out=x_t[:, it * D + dc * 128:it * D + (dc + 1) * 128], in0=ptp[:, 0:128],
                in1=x_t[:, it * D + dc * 128:it * D + (dc + 1) * 128], op=ALU.add),
                reads=[ptpb, x_b], writes=[x_b])
        p.dma("sync", o_x2[:, it * D:(it + 1) * D], x_t[:, it * D:(it + 1) * D],
              reads=[x_b])
    p.finalize(final_waits=p.barrier_tokens())
    return ctx


def build_ffn2():
    ctx = Ctx("ffn2")
    p = ctx.prog
    xn_ap = ctx.dram_in("xn", (128, NDC * T), BF16, None)
    xn2_ap = ctx.dram_in("xn2", (128, NDC * T), BF16, None)
    x_ap = ctx.dram_in("x", (128, NTT * D), F32, None)
    b2a_ap = ctx.dram_in("b2a", (128, NDC * T), F32, None)
    wg_ap = ctx.dram_in("wg", (128, NDC * 8), F32,
                        lambda c, ii: w_fm_view(ii["L2_ffn_Wr"]))
    br_ap = ctx.dram_in("br2", (1, 8), F32, lambda c, ii: ii["L2_ffn_br"].reshape(1, 8))
    w1_ap = [ctx.dram_in(f"w1_{e}", (128, NDC * 4096), F32,
                         lambda c, ii, e=e: w_fm_view(ii["L2_ffn_W1"][e]))
             for e in range(8)]
    b1_ap = ctx.dram_in("b1e", (128, 8 * 32), F32,
                        lambda c, ii: np.concatenate(
                            [vec_fm_view(ii["L2_ffn_b1"][e]) for e in range(8)], axis=1))
    w2_ap = [ctx.dram_in(f"w2_{e}", (128, 32 * D), F32,
                         lambda c, ii, e=e: w_fm_view(ii["L2_ffn_W2"][e]))
             for e in range(8)]
    b2c_ap = ctx.dram_in("b2c", (8, D), F32, lambda c, ii: ii["L2_ffn_b2"])
    idf_ap = ctx.dram_in("idf", (128, 128), F32, lambda c, ii: const_identity_f32())
    o_x3 = ctx.dram_out("o_x3", (128, NTT * D), F32)
    ce_dram = ctx.nc.dram_tensor("ced", [8, D], F32, kind="Internal").ap()

    for nm in ["pma", "pmb", "pr", "pg", "tpf", "pc"]:
        ctx.psum(nm, (128, 512), F32)

    ones_t, ones_b = ctx.sbuf("ones", (128, 128), F32)
    p.op("vector", lambda v: v.memset(ones_t[:, :], 1.0), writes=[ones_b])
    idf_t, idf_b = ctx.sbuf("idf", (128, 128), F32)
    p.dma("sync", idf_t[:, :], idf_ap[:, :], writes=[idf_b])
    xn2_t, xn2_b = ctx.sbuf("xn2", (128, NDC * T), BF16)
    p.dma("sync", xn2_t[:, :], xn2_ap[:, :], writes=[xn2_b])
    wg_t, wg_b = ctx.sbuf("wg", (128, NDC * 8), BF16)
    p.dma("gdma", wg_t[:, :], wg_ap[:, :], writes=[wg_b])
    b1_t, b1_b = ctx.sbuf("b1e", (128, 8 * 32), F32)
    p.dma("sync", b1_t[:, :], b1_ap[:, :], writes=[b1_b])
    br_t, br_b = ctx.sbuf("br2", (1, 8), F32)
    p.dma("sync", br_t[0:1, :], br_ap[:, :], writes=[br_b])
    brr_t, brr_b = ctx.sbuf("brrep", (128, 8), F32)
    pt0, pb0 = ctx.ps["pr"]
    p.op("tensor", lambda t, pt0=pt0: t.matmul(
        out=pt0[:, 0:8], lhsT=ones_t[0:1, :], rhs=br_t[0:1, :],
        start=True, stop=True), reads=[ones_b, br_b], writes=[pb0])
    p.op("vector", lambda v, pt0=pt0: v.tensor_copy(
        out=brr_t[:, :], in_=pt0[:, 0:8]), reads=[pb0], writes=[brr_b])

    comb_t, comb_b = ctx.sbuf("comb", (128, NTT * 8), F32)
    csum = ctx.sbuf("csum", (128, NTT), F32)
    emit_router_b(ctx, (xn2_t, xn2_b), (wg_t, wg_b), (brr_t, brr_b), comb_t,
                  comb_b, "pr", renorm=False, sum_col=csum)

    mT_t, mT_b = ctx.sbuf("mT", (8, NTT * 128), BF16)
    for it in range(NTT):
        ptp, ptpb = ctx.ps["tpf"]
        p.op("tensor", lambda t, it=it, ptp=ptp: t.matmul(
            out=ptp[0:8, 0:128], lhsT=comb_t[:, it * 8:(it + 1) * 8],
            rhs=idf_t[:, :], start=True, stop=True),
            reads=[comb_b, idf_b], writes=[ptpb])
        p.op("vector", lambda v, it=it, ptp=ptp: v.tensor_copy(
            out=mT_t[0:8, it * 128:(it + 1) * 128], in_=ptp[0:8, 0:128]),
            reads=[ptpb], writes=[mT_b])

    acc_t, acc_b = ctx.sbuf("acc", (128, NDC * T), F32)
    for mc in range(NDC):
        p.op("vector", lambda v, mc=mc: v.memset(acc_t[:, mc * T:(mc + 1) * T], 0.0),
             writes=[acc_b])
    re_t, re_b = ctx.sbuf("re", (128, 32 * 1024), BF16)
    rcomb_t, rcomb_b = ctx.sbuf("rcomb", (128, 1024), BF16)
    bt_t, bt_b = ctx.sbuf("btt", (128, 128), F32)
    wmc = [ctx.sbuf(f"wmc{k}", (128, NDC * 128), BF16) for k in range(2)]
    w2m = [ctx.sbuf("w2m0", (128, 32 * 128), BF16)]
    rb1_t, rb1_b = ctx.sbuf("rb1", (128, 32), BF16)
    cer_t, cer_b = ctx.sbuf("cer", (1, D), F32)
    TH = 1024
    for e in range(8):
        p.op("scalar", lambda s, e=e: s.activation(
            out=rb1_t[:, :], in_=b1_t[:, e * 32:(e + 1) * 32], func=AF.Relu,
            scale=1.0), reads=[b1_b], writes=[rb1_b])
        for th in range(2):
            for q in range(8):
                it = th * 8 + q
                p.op("vector", lambda v, it=it, e=e: v.tensor_scalar_mul(
                    out=bt_t[:, :], in0=ones_t[:, :],
                    scalar1=comb_t[:, it * 8 + e:it * 8 + e + 1]),
                    reads=[ones_b], sreads=[comb_b], writes=[bt_b])
                ptp, ptpb = ctx.ps["tpf"]
                p.op("tensor", lambda t, ptp=ptp: t.transpose(
                    out=ptp[:, 0:128], in_=bt_t[:, :], identity=idf_t[:, :]),
                    reads=[bt_b, idf_b], writes=[ptpb])
                p.op("vector", lambda v, q=q, ptp=ptp: v.tensor_copy(
                    out=rcomb_t[:, q * 128:(q + 1) * 128], in_=ptp[:, 0:128]),
                    reads=[ptpb], writes=[rcomb_b])
            for oc in range(32):
                wt, wb = wmc[oc % 2]
                p.dma("gdma", wt[:, :].rearrange("p (z n) -> p z n", z=NDC),
                      w1_ap[e].rearrange("p (z n) -> p z n", z=NDC)[
                          :, :, oc * 128:oc * 128 + 128],
                      writes=[wb])
                for tq in range(2):
                    col0 = th * TH + tq * 512
                    psn = ("pma", "pmb", "pr", "pg")[(oc * 2 + tq) % 4]
                    mm_fm(ctx, psn, wt, wb, 128, NDC, 0, xn2_t, xn2_b, col0, 512)
                    pz, pzb = ctx.ps[psn]
                    p.op("scalar", lambda s, pz=pz, e=e, oc=oc, tq=tq: s.activation(
                        out=sgbuf(ctx)[0][:, 0:512], in_=pz[:, 0:512], func=AF.Relu,
                        bias=b1_t[:, e * 32 + oc:e * 32 + oc + 1], scale=1.0),
                        reads=[pzb], sreads=[b1_b], writes=[sgbuf(ctx)[1]])
                    p.op("vector", lambda v, oc=oc, tq=tq: v.tensor_tensor(
                        out=re_t[:, oc * TH + tq * 512:oc * TH + (tq + 1) * 512],
                        in0=sgbuf(ctx)[0][:, 0:512],
                        in1=rcomb_t[:, tq * 512:(tq + 1) * 512], op=ALU.mult),
                        reads=[sgbuf(ctx)[1], rcomb_b], writes=[re_b])
            for dc in range(NDC):
                wt2, wb2 = w2m[0]
                p.dma("gdma", wt2[:, :].rearrange("p (z n) -> p z n", z=32),
                      w2_ap[e].rearrange("p (z n) -> p z n", z=32)[
                          :, :, dc * 128:dc * 128 + 128],
                      writes=[wb2])
                for tq in range(2):
                    psn = ("pma", "pmb", "pr", "pg")[(dc * 2 + tq) % 4]
                    pz, pzb = ctx.ps[psn]
                    for hc in range(32):
                        tok = p.op("tensor", lambda t, hc=hc, tq=tq, th=th, pz=pz,
                                   wt2=wt2: t.matmul(
                            out=pz[:, 0:512], lhsT=wt2[:, hc * 128:(hc + 1) * 128],
                            rhs=re_t[:, hc * TH + tq * 512:hc * TH + (tq + 1) * 512],
                            start=(hc == 0), stop=(hc == 31)),
                            reads=[wb2, re_b], writes=([pzb] if hc == 0 else []))
                        pzb.writer = tok
                    col = dc * T + th * TH + tq * 512
                    p.op("vector", lambda v, col=col, pz=pz, pzb=pzb: v.tensor_tensor(
                        out=acc_t[:, col:col + 512], in0=pz[:, 0:512],
                        in1=acc_t[:, col:col + 512], op=ALU.add),
                        reads=[pzb, acc_b], writes=[acc_b])
                if th == 0:
                    pc, pcb = ctx.ps["pc"]
                    for hc in range(32):
                        tok = p.op("tensor", lambda t, hc=hc, pc=pc, wt2=wt2: t.matmul(
                            out=pc[0:1, 0:128], lhsT=rb1_t[:, hc:hc + 1],
                            rhs=wt2[:, hc * 128:(hc + 1) * 128],
                            start=(hc == 0), stop=(hc == 31)),
                            reads=[rb1_b, wb2], writes=([pcb] if hc == 0 else []))
                        pcb.writer = tok
                    p.op("vector", lambda v, dc=dc, pc=pc, pcb=pcb: v.tensor_copy(
                        out=cer_t[0:1, dc * 128:(dc + 1) * 128], in_=pc[0:1, 0:128]),
                        reads=[pcb], writes=[cer_b])
        p.dma("sync", ce_dram[e:e + 1, :], cer_t[0:1, :], reads=[cer_b])

    ce_t, ce_b = ctx.sbuf("ce", (8, D), F32)
    p.dma("sync", ce_t[0:8, :], ce_dram[:, :],
          deps=[t for t in [p.dma_out[sx] for sx in p.dma_pool["sync"]] if t],
          writes=[ce_b])
    b2c_t, b2c_b = ctx.sbuf("b2c", (8, D), F32)
    p.dma("sync", b2c_t[0:8, :], b2c_ap[:, :], writes=[b2c_b])
    ctot_t, ctot_b = ctx.sbuf("ctot", (8, D), F32)
    onese_t, onese_b = ctx.sbuf("onese", (8, 1), F32)
    p.op("vector", lambda v: v.memset(onese_t[0:8, :], 1.0), writes=[onese_b])
    ptc, ptcb = ctx.ps["pc"]
    for half in range(2):
        p.op("tensor", lambda t, half=half, ptc=ptc: t.matmul(
            out=ptc[0:1, 0:512], lhsT=onese_t[0:8, 0:1],
            rhs=ce_t[0:8, half * 512:(half + 1) * 512], start=True, stop=True),
            reads=[onese_b, ce_b], writes=[ptcb])
        p.op("vector", lambda v, half=half, ptc=ptc: v.tensor_copy(
            out=ctot_t[0:1, half * 512:(half + 1) * 512], in_=ptc[0:1, 0:512]),
            reads=[ptcb], writes=[ctot_b])
    cpp_t, cpp_b = ctx.sbuf("cpp", (8, D), BF16)
    ptb, ptbb = ctx.ps["pc"]
    p.op("tensor", lambda t, ptb=ptb: t.matmul(
        out=ptb[0:8, 0:512], lhsT=onese_t[0:8, 0:1].rearrange("a b -> b a")
        if False else ones_t[0:1, 0:8], rhs=ctot_t[0:1, 0:512],
        start=True, stop=True), reads=[ones_b, ctot_b], writes=[ptbb])
    p.op("vector", lambda v, ptb=ptb: v.tensor_copy(
        out=cpp_t[0:8, 0:512], in_=ptb[0:8, 0:512]), reads=[ptbb], writes=[cpp_b])
    p.op("tensor", lambda t, ptb=ptb: t.matmul(
        out=ptb[0:8, 0:512], lhsT=ones_t[0:1, 0:8], rhs=ctot_t[0:1, 512:1024],
        start=True, stop=True), reads=[ones_b, ctot_b], writes=[ptbb])
    p.op("vector", lambda v, ptb=ptb: v.tensor_copy(
        out=cpp_t[0:8, 512:1024], in_=ptb[0:8, 0:512]), reads=[ptbb], writes=[cpp_b])
    p.op("vector", lambda v: v.tensor_tensor(
        out=cpp_t[0:8, :], in0=cpp_t[0:8, :], in1=ce_t[0:8, :], op=ALU.subtract),
        reads=[cpp_b, ce_b], writes=[cpp_b])
    p.op("vector", lambda v: v.tensor_tensor(
        out=cpp_t[0:8, :], in0=cpp_t[0:8, :], in1=b2c_t[0:8, :], op=ALU.add),
        reads=[cpp_b, b2c_b], writes=[cpp_b])
    for dc in range(NDC):
        for tt in range(NQT):
            psn = ("pma", "pmb")[tt % 2]
            pz, pzb = ctx.ps[psn]
            p.op("tensor", lambda t, dc=dc, tt=tt, pz=pz: t.matmul(
                out=pz[:, 0:512], lhsT=cpp_t[0:8, dc * 128:(dc + 1) * 128],
                rhs=mT_t[0:8, tt * 512:(tt + 1) * 512], start=True, stop=True),
                reads=[cpp_b, mT_b], writes=[pzb])
            col = dc * T + tt * 512
            p.op("vector", lambda v, col=col, pz=pz, pzb=pzb: v.tensor_tensor(
                out=acc_t[:, col:col + 512], in0=pz[:, 0:512],
                in1=acc_t[:, col:col + 512], op=ALU.add),
                reads=[pzb, acc_b], writes=[acc_b])

    xs_t, xs_b = ctx.sbuf("xs", (128, D), F32)
    sts = [ctx.sbuf(f"sts{k}", (128, 128), F32) for k in range(2)]
    stsb = [ctx.sbuf(f"stsb{k}", (128, 128), BF16) for k in range(2)]
    for it in range(NTT):
        p.dma("sync", xs_t[:, :], x_ap[:, it * D:(it + 1) * D], writes=[xs_b])
        for dc in range(NDC):
            ptp, ptpb = ctx.ps["tpf"]
            p.op("tensor", lambda t, it=it, dc=dc, ptp=ptp: t.transpose(
                out=ptp[:, 0:128],
                in_=acc_t[:, dc * T + it * 128:dc * T + it * 128 + 128],
                identity=idf_t[:, :]), reads=[acc_b, idf_b], writes=[ptpb])
            p.op("vector", lambda v, dc=dc, ptp=ptp, ptpb=ptpb: v.tensor_tensor(
                out=xs_t[:, dc * 128:(dc + 1) * 128], in0=ptp[:, 0:128],
                in1=xs_t[:, dc * 128:(dc + 1) * 128], op=ALU.add),
                reads=[ptpb, xs_b], writes=[xs_b])
            bs_t, bs_b = sts[dc % 2]
            p.dma("sync", bs_t[:, :],
                  b2a_ap[:, dc * T + it * 128:dc * T + it * 128 + 128], writes=[bs_b])
            p.op("tensor", lambda t, ptp=ptp, bs_t=bs_t, bs_b=bs_b: t.transpose(
                out=ptp[:, 0:128], in_=bs_t[:, :], identity=idf_t[:, :]),
                reads=[bs_b, idf_b], writes=[ptpb])
            p.op("vector", lambda v, dc=dc, ptp=ptp, ptpb=ptpb: v.tensor_tensor(
                out=xs_t[:, dc * 128:(dc + 1) * 128], in0=ptp[:, 0:128],
                in1=xs_t[:, dc * 128:(dc + 1) * 128], op=ALU.add),
                reads=[ptpb, xs_b], writes=[xs_b])
            xb_t, xb_b = stsb[dc % 2]
            p.dma("gdma", xb_t[:, :],
                  xn_ap[:, dc * T + it * 128:dc * T + it * 128 + 128],
                  writes=[xb_b])
            idbb = idb_pair(ctx)
            p.op("tensor", lambda t, ptp=ptp, xb_t=xb_t, xb_b=xb_b, idbb=idbb:
                 t.matmul(out=ptp[:, 0:128], lhsT=xb_t[:, :], rhs=idbb[0][:, :],
                          start=True, stop=True),
                 reads=[xb_b, idbb[1]], writes=[ptpb])
            p.op("vector", lambda v, dc=dc, ptp=ptp, ptpb=ptpb: v.tensor_tensor(
                out=xs_t[:, dc * 128:(dc + 1) * 128], in0=ptp[:, 0:128],
                in1=xs_t[:, dc * 128:(dc + 1) * 128], op=ALU.add),
                reads=[ptpb, xs_b], writes=[xs_b])
        p.dma("sync", o_x3[:, it * D:(it + 1) * D], xs_t[:, :], reads=[xs_b])
    p.finalize(final_waits=p.barrier_tokens())
    return ctx


def sgbuf(ctx):
    if "sgbuf" not in ctx.sb:
        ctx.sbuf("sgbuf", (128, 512), BF16)
    return ctx.sb["sgbuf"]


def idb_pair(ctx):
    if "idbp" not in ctx.sb:
        t, b = ctx.sbuf("idbp", (128, 128), BF16)
        idf_t, idf_b = ctx.sb["idf"]
        ctx.prog.op("vector", lambda v: v.tensor_copy(out=t[:, :], in_=idf_t[:, :]),
                    reads=[idf_b], writes=[b])
    return ctx.sb["idbp"]


def emit_router_b(ctx, xnfm, wg_sb, brrep, comb_t, comb_b, psname, renorm,
                  sum_col=None):
    p = ctx.prog
    xnfm_t, xnfm_b = xnfm
    wg_t, wg_b = wg_sb
    brr_t, brr_b = brrep
    pt, pb = ctx.ps[psname]
    lg_t, lg_b = ctx.sbuf("rt_lg", (128, 8), F32)
    st_t, st_b = ctx.sbuf("rt_scr", (128, 16), F32)
    for it in range(NTT):
        for kc in range(NDC):
            tok = p.op("tensor", lambda t, it=it, kc=kc, pt=pt: t.matmul(
                out=pt[:, 0:8], lhsT=xnfm_t[:, kc * T + it * 128:kc * T + it * 128 + 128],
                rhs=wg_t[:, kc * 8:(kc + 1) * 8], start=(kc == 0), stop=(kc == NDC - 1)),
                reads=[xnfm_b, wg_b], writes=([pb] if kc == 0 else []))
            pb.writer = tok
        c0 = it * 8
        sl = comb_t[:, c0:c0 + 8]
        p.op("vector", lambda v, pt=pt: v.tensor_tensor(
            out=lg_t[:, :], in0=pt[:, 0:8], in1=brr_t[:, :], op=ALU.add),
            reads=[pb, brr_b], writes=[lg_b])
        p.op("vector", lambda v: v.tensor_reduce(
            out=st_t[:, 0:1], in_=lg_t[:, :], axis=AX.X, op=ALU.max),
            reads=[lg_b], writes=[st_b])
        p.op("vector", lambda v: v.tensor_scalar_mul(
            out=st_t[:, 1:2], in0=st_t[:, 0:1], scalar1=-1.0),
            reads=[st_b], writes=[st_b])
        p.op("scalar", lambda s, sl=sl: s.activation(
            out=sl, in_=lg_t[:, :], func=AF.Exp, bias=st_t[:, 1:2], scale=1.0),
            reads=[lg_b], sreads=[st_b], writes=[comb_b])
        p.op("vector", lambda v, sl=sl: v.tensor_reduce(
            out=st_t[:, 2:3], in_=sl, axis=AX.X, op=ALU.add),
            reads=[comb_b], writes=[st_b])
        p.op("vector", lambda v: v.reciprocal(
            out=st_t[:, 3:4], in_=st_t[:, 2:3]), reads=[st_b], writes=[st_b])
        p.op("vector", lambda v, sl=sl: v.tensor_scalar_mul(
            out=sl, in0=sl, scalar1=st_t[:, 3:4]),
            reads=[comb_b], sreads=[st_b], writes=[comb_b])
        p.op("vector", lambda v, sl=sl: v.tensor_reduce(
            out=st_t[:, 4:5], in_=sl, axis=AX.X, op=ALU.max),
            reads=[comb_b], writes=[st_b])
        p.op("vector", lambda v, sl=sl: v.tensor_single_scalar(
            out=st_t[:, 8:16], in_=sl, scalar=st_t[:, 4:5], op=ALU.is_ge),
            reads=[comb_b], sreads=[st_b], writes=[st_b])
        p.op("vector", lambda v, sl=sl: v.scalar_tensor_tensor(
            out=st_t[:, 8:16], in0=st_t[:, 8:16], scalar=-1e30, in1=sl,
            op0=ALU.mult, op1=ALU.add), reads=[st_b, comb_b], writes=[st_b])
        p.op("vector", lambda v: v.tensor_reduce(
            out=st_t[:, 5:6], in_=st_t[:, 8:16], axis=AX.X, op=ALU.max),
            reads=[st_b], writes=[st_b])
        p.op("vector", lambda v, sl=sl: v.tensor_single_scalar(
            out=st_t[:, 8:16], in_=sl, scalar=st_t[:, 5:6], op=ALU.is_ge),
            reads=[comb_b], sreads=[st_b], writes=[st_b])
        p.op("vector", lambda v, sl=sl: v.tensor_tensor(
            out=sl, in0=sl, in1=st_t[:, 8:16], op=ALU.mult),
            reads=[comb_b, st_b], writes=[comb_b])
        if sum_col is not None:
            p.op("vector", lambda v, sl=sl, it=it: v.tensor_reduce(
                out=sum_col[0][:, it:it + 1], in_=sl, axis=AX.X, op=ALU.add),
                reads=[comb_b], writes=[sum_col[1]])


def build_ffn3():
    ctx = Ctx("ffn3")
    p = ctx.prog
    xn_ap = ctx.dram_in("xn", (128, NDC * T), BF16, None)
    x_ap = ctx.dram_in("x", (128, NTT * D), F32, None)
    b3a_ap = ctx.dram_in("b3a", (128, NDC * T), F32, None)
    wg_ap = ctx.dram_in("wg", (128, NDC * 8), F32,
                        lambda c, ii: w_fm_view(np.ascontiguousarray(
                            ii["L3_ffn_gate_w"].T)))
    gu_ap = [ctx.dram_in(f"gu_{e}", (128, NDC * 2816), F32,
                         lambda c, ii, e=e: w_fm_view(np.ascontiguousarray(
                             ii["L3_ffn_experts_gate_up"][e].T)))
             for e in range(8)]
    dn_ap = [ctx.dram_in(f"dn_{e}", (128, 11 * D), F32,
                         lambda c, ii, e=e: w_fm_view(np.ascontiguousarray(
                             ii["L3_ffn_experts_down"][e].T)))
             for e in range(8)]
    sg_ap = ctx.dram_in("sgw", (128, NDC * 2816), F32,
                        lambda c, ii: w_fm_view(np.ascontiguousarray(
                            ii["L3_ffn_shared_gate_w"].T)))
    su_ap = ctx.dram_in("suw", (128, NDC * 2816), F32,
                        lambda c, ii: w_fm_view(np.ascontiguousarray(
                            ii["L3_ffn_shared_up_w"].T)))
    sd_ap = ctx.dram_in("sdw", (128, 22 * D), F32,
                        lambda c, ii: w_fm_view(np.ascontiguousarray(
                            ii["L3_ffn_shared_down_w"].T)))
    seg_ap = ctx.dram_in("segw", (128, NDC), F32,
                         lambda c, ii: vec_fm_view(ii["L3_ffn_shared_expert_gate_w"]
                                                   .reshape(D)))
    idf_ap = ctx.dram_in("idf", (128, 128), F32, lambda c, ii: const_identity_f32())
    o_x4 = ctx.dram_out("o_x4", (128, NTT * D), F32)

    for nm in ["pma", "pmb", "pr", "pg", "tpf", "pc"]:
        ctx.psum(nm, (128, 512), F32)

    ones_t, ones_b = ctx.sbuf("ones", (128, 128), F32)
    p.op("vector", lambda v: v.memset(ones_t[:, :], 1.0), writes=[ones_b])
    idf_t, idf_b = ctx.sbuf("idf", (128, 128), F32)
    p.dma("sync", idf_t[:, :], idf_ap[:, :], writes=[idf_b])
    xn_t, xn_b = ctx.sbuf("xn", (128, NDC * T), BF16)
    p.dma("sync", xn_t[:, :], xn_ap[:, :], writes=[xn_b])
    wg_t, wg_b = ctx.sbuf("wg", (128, NDC * 8), BF16)
    p.dma("gdma", wg_t[:, :], wg_ap[:, :], writes=[wg_b])
    seg_t, seg_b = ctx.sbuf("segw", (128, NDC), BF16)
    p.dma("gdma", seg_t[:, :], seg_ap[:, :], writes=[seg_b])

    comb_t, comb_b = ctx.sbuf("comb", (128, NTT * 8), F32)
    emit_router(ctx, (xn_t, xn_b), (wg_t, wg_b), comb_t, comb_b, "pr",
                renorm=False)

    swrow_t, swrow_b = ctx.sbuf("swrow", (1, T), F32)
    ptr, ptrb = ctx.ps["pr"]
    for it in range(NTT):
        for kc in range(NDC):
            tok = p.op("tensor", lambda t, it=it, kc=kc, ptr=ptr: t.matmul(
                out=ptr[0:1, 0:128],
                lhsT=seg_t[:, kc:kc + 1],
                rhs=xn_t[:, kc * T + it * 128:kc * T + it * 128 + 128],
                start=(kc == 0), stop=(kc == NDC - 1)),
                reads=[seg_b, xn_b], writes=([ptrb] if kc == 0 else []))
            ptrb.writer = tok
        p.op("scalar", lambda s, it=it, ptr=ptr: s.activation(
            out=swrow_t[0:1, it * 128:(it + 1) * 128], in_=ptr[0:1, 0:128],
            func=AF.Sigmoid, scale=1.0), reads=[ptrb], writes=[swrow_b])
    rsw_t, rsw_b = ctx.sbuf("rsw", (128, T), F32)
    for tt in range(NQT):
        p.op("tensor", lambda t, tt=tt, ptr=ptr: t.matmul(
            out=ptr[:, 0:512], lhsT=ones_t[0:1, :],
            rhs=swrow_t[0:1, tt * 512:(tt + 1) * 512], start=True, stop=True),
            reads=[ones_b, swrow_b], writes=[ptrb])
        p.op("vector", lambda v, tt=tt, ptr=ptr: v.tensor_copy(
            out=rsw_t[:, tt * 512:(tt + 1) * 512], in_=ptr[:, 0:512]),
            reads=[ptrb], writes=[rsw_b])

    acc_t, acc_b = ctx.sbuf("acc", (128, NDC * T), F32)
    for mc in range(NDC):
        p.op("vector", lambda v, mc=mc: v.memset(acc_t[:, mc * T:(mc + 1) * T], 0.0),
             writes=[acc_b])

    glu_t, glu_b = ctx.sbuf("glu", (128, 11 * T), BF16)
    sil_t, sil_b = ctx.sbuf("sil", (128, 512), F32)
    rc_t, rc_b = ctx.sbuf("rcw", (128, T), F32)
    bt_t, bt_b = ctx.sbuf("btt", (128, 128), F32)
    wmc = [ctx.sbuf(f"wmc{k}", (128, NDC * 128), BF16) for k in range(2)]
    wdn = [ctx.sbuf(f"wdn{k}", (128, 11 * 128), BF16) for k in range(2)]

    def expert_pass(gu_src, dn_src, weight_mode, e=None):
        if weight_mode == "comb":
            for q in range(NTT):
                p.op("vector", lambda v, q=q, e=e: v.tensor_scalar_mul(
                    out=bt_t[:, :], in0=ones_t[:, :],
                    scalar1=comb_t[:, q * 8 + e:q * 8 + e + 1]),
                    reads=[ones_b], sreads=[comb_b], writes=[bt_b])
                ptp, ptpb = ctx.ps["tpf"]
                p.op("tensor", lambda t, ptp=ptp: t.transpose(
                    out=ptp[:, 0:128], in_=bt_t[:, :], identity=idf_t[:, :]),
                    reads=[bt_b, idf_b], writes=[ptpb])
                p.op("vector", lambda v, q=q, ptp=ptp, ptpb=ptpb: v.tensor_copy(
                    out=rc_t[:, q * 128:(q + 1) * 128], in_=ptp[:, 0:128]),
                    reads=[ptpb], writes=[rc_b])
        for j in range(11):
            for which, oc in ((0, j), (1, j + 11)):
                wt, wb = wmc[oc % 2]
                p.dma("gdma", wt[:, :].rearrange("p (z n) -> p z n", z=NDC),
                      gu_src.rearrange("p (z n) -> p z n", z=NDC)[
                          :, :, oc * 128:oc * 128 + 128],
                      writes=[wb])
                for tt in range(NQT):
                    psn = ("pma", "pmb", "pr", "pg")[tt % 4]
                    mm_fm(ctx, psn, wt, wb, 128, NDC, 0, xn_t, xn_b, tt * 512, 512)
                    pz, pzb = ctx.ps[psn]
                    if which == 0:
                        p.op("scalar", lambda s, pz=pz, tt=tt: s.activation(
                            out=sil_t[:, 0:512], in_=pz[:, 0:512], func=AF.Silu,
                            scale=1.0), reads=[pzb], writes=[sil_b])
                        p.op("vector", lambda v, j=j, tt=tt: v.tensor_copy(
                            out=glu_t[:, j * T + tt * 512:j * T + (tt + 1) * 512],
                            in_=sil_t[:, 0:512]), reads=[sil_b], writes=[glu_b])
                    else:
                        if weight_mode == "comb":
                            p.op("vector", lambda v, j=j, tt=tt, pz=pz, pzb=pzb:
                                 v.tensor_tensor(
                                     out=sil_t[:, 0:512], in0=pz[:, 0:512],
                                     in1=glu_t[:, j * T + tt * 512:j * T + (tt + 1) * 512],
                                     op=ALU.mult),
                                 reads=[pzb, glu_b], writes=[sil_b])
                            p.op("vector", lambda v, j=j, tt=tt: v.tensor_tensor(
                                 out=glu_t[:, j * T + tt * 512:j * T + (tt + 1) * 512],
                                 in0=sil_t[:, 0:512],
                                 in1=rc_t[:, tt * 512:(tt + 1) * 512], op=ALU.mult),
                                 reads=[sil_b, rc_b], writes=[glu_b])
                        else:
                            p.op("vector", lambda v, j=j, tt=tt, pz=pz, pzb=pzb:
                                 v.tensor_tensor(
                                     out=glu_t[:, j * T + tt * 512:j * T + (tt + 1) * 512],
                                     in0=pz[:, 0:512],
                                     in1=glu_t[:, j * T + tt * 512:j * T + (tt + 1) * 512],
                                     op=ALU.mult),
                                 reads=[pzb, glu_b], writes=[glu_b])
        nhc = 11
        for dc in range(NDC):
            wt2, wb2 = wdn[dc % 2]
            p.dma("gdma", wt2[:, 0:nhc * 128].rearrange("p (z n) -> p z n", z=nhc),
                  dn_src.rearrange("p (z n) -> p z n", z=nhc)[
                      :, :, dc * 128:dc * 128 + 128],
                  writes=[wb2])
            for tt in range(NQT):
                psn = ("pma", "pmb", "pr", "pg")[tt % 4]
                pz, pzb = ctx.ps[psn]
                for hc in range(nhc):
                    tok = p.op("tensor", lambda t, hc=hc, tt=tt, pz=pz, wt2=wt2:
                               t.matmul(
                        out=pz[:, 0:512], lhsT=wt2[:, hc * 128:(hc + 1) * 128],
                        rhs=glu_t[:, hc * T + tt * 512:hc * T + (tt + 1) * 512],
                        start=(hc == 0), stop=(hc == nhc - 1)),
                        reads=[wb2, glu_b], writes=([pzb] if hc == 0 else []))
                    pzb.writer = tok
                col = dc * T + tt * 512
                if weight_mode == "sw":
                    p.op("vector", lambda v, col=col, tt=tt, pz=pz, pzb=pzb:
                         v.tensor_tensor(
                             out=sil_t[:, 0:512], in0=pz[:, 0:512],
                             in1=rsw_t[:, tt * 512:(tt + 1) * 512], op=ALU.mult),
                         reads=[pzb, rsw_b], writes=[sil_b])
                    p.op("vector", lambda v, col=col: v.tensor_tensor(
                        out=acc_t[:, col:col + 512], in0=sil_t[:, 0:512],
                        in1=acc_t[:, col:col + 512], op=ALU.add),
                        reads=[sil_b, acc_b], writes=[acc_b])
                else:
                    p.op("vector", lambda v, col=col, pz=pz, pzb=pzb: v.tensor_tensor(
                        out=acc_t[:, col:col + 512], in0=pz[:, 0:512],
                        in1=acc_t[:, col:col + 512], op=ALU.add),
                        reads=[pzb, acc_b], writes=[acc_b])

    for e in range(8):
        expert_pass(gu_ap[e], dn_ap[e], "comb", e)

    glu2_t, glu2_b = glu_t, glu_b
    TH = 1024
    for th in range(2):
        for j in range(22):
            for which, src_ap in ((0, sg_ap), (1, su_ap)):
                wt, wb = wmc[(j + which) % 2]
                p.dma("gdma", wt[:, :].rearrange("p (z n) -> p z n", z=NDC),
                      src_ap.rearrange("p (z n) -> p z n", z=NDC)[
                          :, :, j * 128:j * 128 + 128],
                      writes=[wb])
                for tq in range(2):
                    col0 = th * TH + tq * 512
                    psn = ("pma", "pmb")[tq % 2]
                    mm_fm(ctx, psn, wt, wb, 128, NDC, 0, xn_t, xn_b, col0, 512)
                    pz, pzb = ctx.ps[psn]
                    if which == 0:
                        p.op("scalar", lambda s, pz=pz: s.activation(
                            out=sil_t[:, 0:512], in_=pz[:, 0:512], func=AF.Silu,
                            scale=1.0), reads=[pzb], writes=[sil_b])
                        p.op("vector", lambda v, j=j, tq=tq: v.tensor_copy(
                            out=glu2_t[:, j * TH + tq * 512:j * TH + (tq + 1) * 512],
                            in_=sil_t[:, 0:512]), reads=[sil_b], writes=[glu2_b])
                    else:
                        p.op("vector", lambda v, j=j, tq=tq, pz=pz, pzb=pzb:
                             v.tensor_tensor(
                                 out=glu2_t[:, j * TH + tq * 512:j * TH + (tq + 1) * 512],
                                 in0=pz[:, 0:512],
                                 in1=glu2_t[:, j * TH + tq * 512:j * TH + (tq + 1) * 512],
                                 op=ALU.mult),
                             reads=[pzb, glu2_b], writes=[glu2_b])
        for dc in range(NDC):
            wt2, wb2 = wdn[dc % 2]
            p.dma("gdma", wt2[:, 0:22 * 128].rearrange("p (z n) -> p z n", z=22)
                  if False else wt2[:, :].rearrange("p (z n) -> p z n", z=11),
                  sd_ap.rearrange("p (z n) -> p z n", z=22)[
                      :, 0:11, dc * 128:dc * 128 + 128],
                  writes=[wb2])
            for hh in range(2):
                if hh == 1:
                    p.dma("gdma", wt2[:, :].rearrange("p (z n) -> p z n", z=11),
                          sd_ap.rearrange("p (z n) -> p z n", z=22)[
                              :, 11:22, dc * 128:dc * 128 + 128],
                          writes=[wb2])
                for tq in range(2):
                    psn = ("pma", "pmb")[tq % 2]
                    pz, pzb = ctx.ps[psn]
                    for hc in range(11):
                        ghc = hh * 11 + hc
                        tok = p.op("tensor", lambda t, hc=hc, ghc=ghc, tq=tq, th=th,
                                   hh=hh, pz=pz, wt2=wt2: t.matmul(
                            out=pz[:, 0:512], lhsT=wt2[:, hc * 128:(hc + 1) * 128],
                            rhs=glu2_t[:, ghc * TH + tq * 512:ghc * TH + (tq + 1) * 512],
                            start=(hh == 0 and hc == 0), stop=(hh == 1 and hc == 10)),
                            reads=[wb2, glu2_b], writes=([pzb] if (hh == 0 and hc == 0)
                                                         else []))
                        pzb.writer = tok
                    if hh == 1:
                        col = dc * T + th * TH + tq * 512
                        p.op("vector", lambda v, col=col, th=th, tq=tq, pz=pz,
                             pzb=pzb: v.tensor_tensor(
                                 out=sil_t[:, 0:512], in0=pz[:, 0:512],
                                 in1=rsw_t[:, th * TH + tq * 512:th * TH + (tq + 1) * 512],
                                 op=ALU.mult),
                             reads=[pzb, rsw_b], writes=[sil_b])
                        p.op("vector", lambda v, col=col: v.tensor_tensor(
                            out=acc_t[:, col:col + 512], in0=sil_t[:, 0:512],
                            in1=acc_t[:, col:col + 512], op=ALU.add),
                            reads=[sil_b, acc_b], writes=[acc_b])

    xs_t, xs_b = ctx.sbuf("xs", (128, D), F32)
    sts = [ctx.sbuf(f"sts{k}", (128, 128), F32) for k in range(2)]
    for it in range(NTT):
        p.dma("sync", xs_t[:, :], x_ap[:, it * D:(it + 1) * D], writes=[xs_b])
        for dc in range(NDC):
            ptp, ptpb = ctx.ps["tpf"]
            p.op("tensor", lambda t, it=it, dc=dc, ptp=ptp: t.transpose(
                out=ptp[:, 0:128],
                in_=acc_t[:, dc * T + it * 128:dc * T + it * 128 + 128],
                identity=idf_t[:, :]), reads=[acc_b, idf_b], writes=[ptpb])
            p.op("vector", lambda v, dc=dc, ptp=ptp, ptpb=ptpb: v.tensor_tensor(
                out=xs_t[:, dc * 128:(dc + 1) * 128], in0=ptp[:, 0:128],
                in1=xs_t[:, dc * 128:(dc + 1) * 128], op=ALU.add),
                reads=[ptpb, xs_b], writes=[xs_b])
            bs_t, bs_b = sts[dc % 2]
            p.dma("sync", bs_t[:, :],
                  b3a_ap[:, dc * T + it * 128:dc * T + it * 128 + 128], writes=[bs_b])
            p.op("tensor", lambda t, ptp=ptp, bs_t=bs_t, bs_b=bs_b: t.transpose(
                out=ptp[:, 0:128], in_=bs_t[:, :], identity=idf_t[:, :]),
                reads=[bs_b, idf_b], writes=[ptpb])
            p.op("vector", lambda v, dc=dc, ptp=ptp, ptpb=ptpb: v.tensor_tensor(
                out=xs_t[:, dc * 128:(dc + 1) * 128], in0=ptp[:, 0:128],
                in1=xs_t[:, dc * 128:(dc + 1) * 128], op=ALU.add),
                reads=[ptpb, xs_b], writes=[xs_b])
        p.dma("sync", o_x4[:, it * D:(it + 1) * D], xs_t[:, :], reads=[xs_b])
    p.finalize(final_waits=p.barrier_tokens())
    return ctx



def _toktiled_to_full(res_key, results):
    out = np.zeros((2, S, D), np.float32)
    for c in range(NCORE):
        b, q = c // 4, c % 4
        out[b, q * T:(q + 1) * T] = results[c][res_key].reshape(128, NTT, D) \
            .transpose(1, 0, 2).reshape(T, D)
    return out


def kernel(**inputs):
    ii = dict(inputs)

    pre = run_ctx(build_pre(0, x_source_from_key("L0_attn_x")), ii)
    ares = run_ctx(finish_attn(build_attn(0), 0), ii, extra=attn_extra(0, pre, ii))
    mfres = run_ctx(build_mamba_front(), ii, extra=l0_host_chain(ii, pre, ares))
    sares = run_ctx(build_scan_a(), ii, extra=scan_extra(ii, pre, ares, mfres))
    mares = run_ctx(build_mlp_assemble(), ii, extra={
        "z2": [sares[c]["o_z2"] for c in range(NCORE)],
        "ffn": [pre[c]["o_ffn"] for c in range(NCORE)]})
    x_cur = [mares[c]["o_x1"] for c in range(NCORE)]

    for layer, ffn_builder in ((1, build_ffn1), (2, build_ffn2), (3, build_ffn3)):
        pre = run_ctx(build_pre(layer, lambda c, _ii, xs=x_cur: xs[c]), ii)
        ares = run_ctx(finish_attn(build_attn(layer), layer), ii,
                       extra=attn_extra(layer, pre, ii))
        extra = {
            "xn": [pre[c]["o_xn"] for c in range(NCORE)],
            "x": x_cur,
            ("b1a" if layer == 1 else "b2a" if layer == 2 else "b3a"):
                [ares[c]["o_z"] for c in range(NCORE)],
        }
        if layer == 2:
            extra["xn2"] = [pre[c]["o_xn2"] for c in range(NCORE)]
        fres = run_ctx(ffn_builder(), ii, extra=extra)
        okey = {1: "o_x2", 2: "o_x3", 3: "o_x4"}[layer]
        x_cur = [fres[c][okey] for c in range(NCORE)]

    out = np.zeros((2, S, D), np.float32)
    for c in range(NCORE):
        b, q = c // 4, c % 4
        out[b, q * T:(q + 1) * T] = x_cur[c].reshape(128, NTT, D) \
            .transpose(1, 0, 2).reshape(T, D)
    return out
